# version 20
# baseline (speedup 1.0000x reference)
import contextlib
import numpy as np
import concourse.bass as bass
import concourse.mybir as mybir
from concourse.bass_utils import run_bass_kernel_spmd

F32 = mybir.dt.float32
BF16 = mybir.dt.bfloat16
I32 = mybir.dt.int32
AF = mybir.ActivationFunctionType
ALU = mybir.AluOpType
AX = mybir.AxisListType

EPS = 1e-5
CFG_FULL = dict(D=2048, T=2048, TO=1024, FF=8192, R=16)


class Sem:
    def __init__(self, handle):
        self.h = handle
        self.n = 0


def build_program(cfg, debug_outputs=False):
    nc = bass.Bass("TRN2", target_bir_lowering=False)
    scratch_kind = "ExternalOutput" if debug_outputs else "Internal"
    D, T, TO, FF, R = cfg["D"], cfg["T"], cfg["TO"], cfg["FF"], cfg["R"]
    DC, FC, TC = D // 128, FF // 128, T // 128
    RSQH = float(1.0 / np.sqrt(D))

    def blocks(n):
        out, i = [], 0
        while i < n:
            out.append((i, min(512, n - i)))
            i += 512
        return out

    TBS = blocks(TO)
    KTB = blocks(T)
    QBS = min(512, TO)
    NQB = TO // QBS
    GS = min(8, FC)
    NG = FC // GS
    PA = min(256, D)
    NPAN = 3 * D // PA

    dram = nc.dram_tensor
    xT = dram("xT", [D, T], F32, kind="ExternalInput")
    wqkvT = dram("wqkvT", [D, 3 * D], F32, kind="ExternalInput")
    woutT = dram("woutT", [D, D], F32, kind="ExternalInput")
    bqkv_pp = dram("bqkv_pp", [128, 3 * DC], F32, kind="ExternalInput")
    bout_pp = dram("bout_pp", [128, DC], F32, kind="ExternalInput")
    wupqT = dram("wupqT", [D, FF], I32, kind="ExternalInput")
    sexpu = dram("sexpu", [D, FF], F32, kind="ExternalInput")
    bup_pp = dram("bup_pp", [128, FC], F32, kind="ExternalInput")
    aup = dram("aup", [D, R], F32, kind="ExternalInput")
    bupl = dram("bupl", [R, FF], F32, kind="ExternalInput")
    wdnqT = dram("wdnqT", [FF, D], I32, kind="ExternalInput")
    sexpd = dram("sexpd", [FF, D], F32, kind="ExternalInput")
    bdn_pp = dram("bdn_pp", [128, DC], F32, kind="ExternalInput")
    adn = dram("adn", [FF, R], F32, kind="ExternalInput")
    bdnl = dram("bdnl", [R, D], F32, kind="ExternalInput")
    ident = dram("ident", [128, 128], F32, kind="ExternalInput")
    y = dram("y", [D, TO], F32, kind="ExternalOutput")
    q_scr = dram("q_scr", [D, TO], BF16, kind=scratch_kind)
    if debug_outputs:
        ctx_scr = dram("ctx_scr", [D, TO], BF16, kind="ExternalOutput")
        h1_scr = dram("h1_scr", [D, TO], F32, kind="ExternalOutput")
        pb_scr = dram("pb_scr", [TO, T], BF16, kind="ExternalOutput")
    v_scr = dram("v_scr", [T, D], BF16, kind=scratch_kind)
    x1_scr = dram("x1_scr", [D, TO], F32, kind=scratch_kind)

    LIMIT = 211200
    ARENA = nc.bump_sbuf(LIMIT)[0]
    bump = [0]

    def sb(name, cols, dtype, at=None):
        esz = 2 if dtype == BF16 else 4
        if at is None:
            at = bump[0]
            bump[0] = at + ((cols * esz + 31) // 32) * 32
            assert bump[0] <= LIMIT, f"{name} overflows sbuf ({bump[0]})"
            end = bump[0]
        else:
            end = at + ((cols * esz + 31) // 32) * 32
            assert end <= LIMIT, f"{name} overflows sbuf ({end})"
        t = nc.alloc_sbuf_tensor_at(name, [128, cols], dtype, offset=ARENA + at)
        return t, end

    def sbp(name, cols, dtype):
        t, _ = sb(name, cols, dtype)
        return t

    s_bq = sbp("s_bq", 3 * DC, F32)
    s_bo = sbp("s_bo", DC, F32)
    s_bu = sbp("s_bu", FC, F32)
    s_bd = sbp("s_bd", DC, F32)
    s_aup = sbp("s_aup", DC * R, BF16)
    s_adn = sbp("s_adn", FC * R, BF16)
    s_t1 = sbp("s_t1", TO, BF16)
    s_t2 = sbp("s_t2", TO, BF16)
    s_ones1 = sbp("s_ones1", 128, F32)
    s_onesb = sbp("s_onesb", 16, BF16)
    s_id = sbp("s_id", 128, BF16)
    s_rows = sbp("s_rows", 512 * 3, F32)
    s_h1b = sbp("s_h1b", 512 * 2, BF16)
    s_h1sq = sbp("s_h1sq", 512 * 2, BF16)
    s_tmp = sbp("s_tmp", 512, F32)
    s_negm = sbp("s_negm", 8, F32)
    s_oev = sbp("s_oev", 512 * 2, F32)
    s_bcast = sbp("s_bcast", 512 * 2, F32)

    P0 = ((bump[0] + 63) // 64) * 64

    s_K, pend = sb("s_K", DC * T, BF16, at=P0)
    PK = ((pend + 63) // 64) * 64

    s_xbf, a1 = sb("s_xbf", DC * T, BF16, at=PK)
    s_wq0, a2 = sb("s_wq0", DC * PA, BF16, at=a1)
    s_wq1, a3 = sb("s_wq1", DC * PA, BF16, at=a2)
    s_qev0, a4 = sb("s_qev0", 512, BF16, at=a3)
    s_qev1, a5 = sb("s_qev1", 512, BF16, at=a4)

    QW = 128
    s_ctx, b1 = sb("s_ctx", DC * QBS, BF16, at=PK)
    s_q0, _ = sb("s_q0", DC * QW, BF16, at=PK)
    s_q1, _q1e = sb("s_q1", DC * QW, BF16, at=PK + ((DC * QW * 2 + 31) // 32) * 32)
    s_probs, b2 = sb("s_probs", T, BF16, at=b1)
    s_probsT, b3 = sb("s_probsT", TC * QBS, BF16, at=b2)
    s_vp0, b4 = sb("s_vp0", TC * 256, BF16, at=b3)
    s_vp1, b5 = sb("s_vp1", TC * 256, BF16, at=b4)
    s_wo0, b5b = sb("s_wo0", DC * 256, BF16, at=b5)
    s_wo1, b6 = sb("s_wo1", DC * 256, BF16, at=b5b)
    s_xow0, b7 = sb("s_xow0", QBS, F32, at=b6)
    s_xow1, b8 = sb("s_xow1", QBS, F32, at=b7)
    s_xln, b9 = sb("s_xln", DC * QBS, F32, at=b8)
    s_scores, b10 = sb("s_scores", T, F32, at=b9)

    s_acc, c1 = sb("s_acc", DC * TO, F32, at=P0)
    s_xlnb, c2 = sb("s_xlnb", DC * TO, BF16, at=c1)
    s_x2, c3 = sb("s_x2", GS * TO, BF16, at=c2)
    s_uq0, c4 = sb("s_uq0", DC * 128, BF16, at=c3)
    s_uq1, c5 = sb("s_uq1", DC * 128, BF16, at=c4)
    s_uq2, c5b = sb("s_uq2", DC * 128, BF16, at=c5)
    s_us0, c6 = sb("s_us0", DC * 128, BF16, at=c5b)
    s_us1, c7 = sb("s_us1", DC * 128, BF16, at=c6)
    s_us2, c7b = sb("s_us2", DC * 128, BF16, at=c7)
    s_dq0, c8 = sb("s_dq0", GS * 128, BF16, at=c7b)
    s_dq1, c9 = sb("s_dq1", GS * 128, BF16, at=c8)
    s_dq2, c9b = sb("s_dq2", GS * 128, BF16, at=c9)
    s_ds0, c10 = sb("s_ds0", GS * 128, BF16, at=c9b)
    s_ds1, c11 = sb("s_ds1", GS * 128, BF16, at=c10)
    s_ds2, c11b = sb("s_ds2", GS * 128, BF16, at=c11)
    s_bupl, c12 = sb("s_bupl", FF, BF16, at=c11b)
    s_bdnl, c13 = sb("s_bdnl", D, BF16, at=c12)

    ops = {k: [] for k in ("t", "v", "a", "g", "s")}

    def E(eng, fn, waits=(), inc=None, inc_by=1):
        incs = []
        if inc is not None:
            inc.n += inc_by
            incs.append((inc.h, inc_by))
        ops[eng].append((fn, [(s.h, v) for s, v in waits if v > 0], incs))

    def DMA(eng, sem, dst, src, waits=()):
        E(eng, lambda e: e.dma_start(dst, src), waits, inc=sem, inc_by=16)

    with contextlib.ExitStack() as stack:
        en = stack.enter_context
        pp = [en(nc.psum_tensor(f"pp{i}", [128, 512], F32)) for i in range(4)]
        p_tr = [en(nc.psum_tensor("p_tr0", [128, 128], BF16)),
                en(nc.psum_tensor("p_tr1", [128, 128], BF16))]
        pst = [en(nc.psum_tensor(f"pst{i}", [16, 512], F32)) for i in range(2)]
        def S(name):
            return Sem(en(nc.semaphore(name)))
        pe, act, dve, po = S("pe"), S("act"), S("dve"), S("po")
        dpre, dx = S("dpre"), S("dx")
        dwq = [S("dwq0"), S("dwq1")]
        dqs = [S("dqs0"), S("dqs1")]
        dqr = [S("dqr0"), S("dqr1")]
        dvp = [S("dvp0"), S("dvp1")]
        dwo = [S("dwo0"), S("dwo1")]
        dxw = [S("dxw0"), S("dxw1")]
        dso = [S("dso0"), S("dso1")]
        dacc = S("dacc")
        duq = [S("duq0"), S("duq1"), S("duq2")]
        dus = [S("dus0"), S("dus1"), S("dus2")]
        ddn = [S("ddn0"), S("ddn1"), S("ddn2")]
        dds = [S("dds0"), S("dds1"), S("dds2")]
        dlb = S("dlb")

        wq_bufs = (s_wq0, s_wq1)
        qev_bufs = (s_qev0, s_qev1)
        xow_bufs = (s_xow0, s_xow1)
        uq_bufs, us_bufs = (s_uq0, s_uq1, s_uq2), (s_us0, s_us1, s_us2)
        dq_bufs, ds_bufs = (s_dq0, s_dq1, s_dq2), (s_ds0, s_ds1, s_ds2)
        vp_bufs = (s_vp0, s_vp1)
        wo_bufs = (s_wo0, s_wo1)

        bank_free = {}

        def run_ln(*args, **kw):
            res = {}
            for _ in layernorm_gen(*args, result=res, **kw):
                pass
            return res

        def layernorm_gen(src, tb0, tn, ready_dve, out_dram, out_col0, src_stride,
                          acc_dst=None, acc_stride=None, result=None,
                          ready_map=None):
            MEAN, VARC, STDC = 0, 512, 1024
            invD = 1.0 / D
            for dc in range(DC):
                par = dc % 2
                if ready_map is not None:
                    w = [ready_map[dc], (pe, bank_free.get(("lnst", par), 0))]
                else:
                    w = ([(dve, ready_dve)] if dc == 0 else []) + \
                        [(pe, bank_free.get(("lnst", par), 0))]
                E("a", (lambda e, dc=dc, par=par:
                        e.activation(s_h1b[:, par * 512: par * 512 + tn],
                                     src[:, dc * src_stride + tb0: dc * src_stride + tb0 + tn],
                                     AF.Copy)), waits=w, inc=act)
                E("a", (lambda e, dc=dc, par=par:
                        e.activation(s_h1sq[:, par * 512: par * 512 + tn],
                                     src[:, dc * src_stride + tb0: dc * src_stride + tb0 + tn],
                                     AF.Square)), inc=act)
                act_h = act.n
                E("t", (lambda e, dc=dc, par=par:
                        e.matmul(pst[0][0:1, 0:tn], s_onesb[:, 0:1],
                                 s_h1b[:, par * 512: par * 512 + tn],
                                 start=(dc == 0), stop=(dc == DC - 1),
                                 skip_group_check=True)),
                  waits=([(act, act_h)] +
                         ([(dve, bank_free.get("lnrows", 0))] if dc == 0 else [])))
                E("t", (lambda e, dc=dc, par=par:
                        e.matmul(pst[1][0:1, 0:tn], s_onesb[:, 0:1],
                                 s_h1sq[:, par * 512: par * 512 + tn],
                                 start=(dc == 0), stop=(dc == DC - 1),
                                 skip_group_check=True)), inc=pe)
                bank_free[("lnst", par)] = pe.n
                if dc % 4 == 3:
                    yield
            pe_stats = pe.n

            E("v", (lambda e: e.tensor_scalar_mul(
                s_rows[0:1, MEAN:MEAN + tn], pst[0][0:1, 0:tn], invD)),
              waits=[(pe, pe_stats)], inc=dve)
            E("v", (lambda e: e.tensor_scalar_mul(
                s_rows[0:1, VARC:VARC + tn], pst[1][0:1, 0:tn], invD)), inc=dve)
            E("a", (lambda e: e.activation(
                s_rows[0:1, STDC:STDC + tn], s_rows[0:1, MEAN:MEAN + tn], AF.Square)),
              waits=[(dve, dve.n)], inc=act)
            E("v", (lambda e: e.tensor_sub(
                s_rows[0:1, VARC:VARC + tn], s_rows[0:1, VARC:VARC + tn],
                s_rows[0:1, STDC:STDC + tn])), waits=[(act, act.n)], inc=dve)
            E("a", (lambda e: e.activation(
                s_rows[0:1, STDC:STDC + tn], s_rows[0:1, VARC:VARC + tn],
                AF.Sqrt, bias=s_negm[0:1, 7:8])), waits=[(dve, dve.n)], inc=act)
            E("v", (lambda e: e.reciprocal(
                s_rows[0:1, VARC:VARC + tn], s_rows[0:1, STDC:STDC + tn])),
              waits=[(act, act.n)], inc=dve)
            E("v", (lambda e: e.tensor_mul(
                s_rows[0:1, STDC:STDC + tn], s_rows[0:1, MEAN:MEAN + tn],
                s_rows[0:1, VARC:VARC + tn])), waits=[(dve, dve.n)], inc=dve)
            dve_rows = dve.n
            bank_free["lnrows"] = dve_rows

            yield
            E("t", (lambda e: e.matmul(pp[2][:, 0:tn], s_ones1[0:1, :],
                                       s_rows[0:1, VARC:VARC + tn],
                                       start=True, stop=True, skip_group_check=True)),
              waits=[(dve, max(dve_rows, bank_free.get(("lnbc", 0), 0),
                               bank_free.get(("ppV", 2), 0))),
                     (act, max(bank_free.get(("wo", 2), 0),
                               bank_free.get(("ppA", 2), 0)))])
            E("t", (lambda e: e.matmul(pp[3][:, 0:tn], s_ones1[0:1, :],
                                       s_rows[0:1, STDC:STDC + tn],
                                       start=True, stop=True, skip_group_check=True)),
              waits=[(dve, max(bank_free.get(("lnbc", 1), 0),
                               bank_free.get(("ppV", 3), 0))),
                     (act, bank_free.get(("ppA", 3), 0))], inc=pe)
            pe_bc = pe.n
            E("v", (lambda e: e.tensor_copy(s_bcast[:, 0:tn], pp[2][:, 0:tn])),
              waits=[(pe, pe_bc), (po, bank_free.get("ln_po", 0))], inc=dve)
            E("v", (lambda e: e.tensor_copy(s_bcast[:, 512:512 + tn], pp[3][:, 0:tn])),
              waits=[(dve, dve.n)], inc=dve)
            dve_bc = dve.n
            bank_free[("lnbc", 0)] = dve_bc
            bank_free[("lnbc", 1)] = dve_bc

            apply_dve = {}
            for dc in range(DC):
                par = dc % 2
                eng = "v"
                esem = dve
                if acc_dst is None:
                    dst = s_oev[:, par * 512: par * 512 + tn]
                else:
                    dst = acc_dst[:, dc * acc_stride + out_col0:
                                  dc * acc_stride + out_col0 + tn]
                w_mul = [(dve, dve_bc), (esem, esem.n)]
                if acc_dst is None:
                    w_mul.append((dso[par], bank_free.get(("lnout", par), 0)))
                E(eng, (lambda e, dc=dc, dst=dst:
                        e.tensor_mul(dst,
                                     src[:, dc * src_stride + tb0: dc * src_stride + tb0 + tn],
                                     s_bcast[:, 0:tn])),
                  waits=w_mul, inc=esem)
                E(eng, (lambda e, dst=dst:
                        e.tensor_sub(dst, dst, s_bcast[:, 512:512 + tn])),
                  waits=[(esem, esem.n)], inc=esem)
                if acc_dst is None:
                    DMA("s", dso[par],
                        out_dram[dc * 128:(dc + 1) * 128, out_col0:out_col0 + tn],
                        s_oev[:, par * 512: par * 512 + tn], waits=[(esem, esem.n)])
                    bank_free[("lnout", par)] = dso[par].n
                apply_dve[dc] = dve.n
            bank_free["ln_po"] = po.n
            if result is not None:
                result.update(apply_dve)

        E("v", lambda e: e.memset(s_t1[:, :], 0.0), inc=dve)
        E("v", lambda e: e.memset(s_t2[:, :], 0.0), inc=dve)
        E("v", lambda e: e.memset(s_ones1[:, :], 0.0), inc=dve)
        E("v", lambda e: e.memset(s_rows[:, :], 0.0), inc=dve)
        E("v", lambda e: e.memset(s_ones1[0:1, :], 1.0), waits=[(dve, dve.n)], inc=dve)
        E("v", lambda e: e.memset(s_onesb[:, 0:1], 1.0), inc=dve)
        E("v", lambda e: e.memset(s_negm[:, 7:8], EPS), inc=dve)
        pre_dve = dve.n

        DMA("g", dpre, s_bq[:, :], bqkv_pp[:, :])
        DMA("g", dpre, s_bo[:, :], bout_pp[:, :])
        DMA("g", dpre, s_bu[:, :], bup_pp[:, :])
        DMA("g", dpre, s_bd[:, :], bdn_pp[:, :])
        DMA("g", dpre, s_id[:, :], ident[:, :])
        DMA("g", dpre, s_aup[:, :].rearrange("p (c r) -> p c r", r=R),
            aup[:, :].rearrange("(c p) r -> p c r", p=128))
        DMA("g", dpre, s_adn[:, :].rearrange("p (c r) -> p c r", r=R),
            adn[:, :].rearrange("(c p) r -> p c r", p=128))
        pre_all = dpre.n

        dx_at = {}
        for dc in range(DC):
            DMA("g", dx, s_xbf[:, dc * T:(dc + 1) * T], xT[dc * 128:(dc + 1) * 128, :])
            dx_at[dc] = dx.n
        dx_all = dx.n

        wqv = [b[:, :].rearrange("p (c o) -> p c o", o=PA) for b in wq_bufs]
        wqkv_v = wqkvT[:, :].rearrange("(c p) o -> p c o", p=128)

        pan_dg, pan_pe_done = {}, {}
        psA_free = {0: 0, 1: 0}
        qev_free = {0: 0, 1: 0}
        evi = [0]

        def load_panel(p):
            w = [(pe, pan_pe_done[p - 2])] if p - 2 in pan_pe_done else []
            DMA("g", dwq[p % 2], wqv[p % 2][:, :, :],
                wqkv_v[:, :, p * PA:(p + 1) * PA], waits=w)
            pan_dg[p] = dwq[p % 2].n

        load_panel(0)
        if NPAN > 1:
            load_panel(1)

        for p in range(NPAN):
            buf = wq_bufs[p % 2]
            for oc in range(PA // 128):
                o_g = p * PA + oc * 128
                if o_g < 2 * D:
                    is_q = o_g < D
                    for (t0, tn) in (TBS if is_q else KTB):
                        bank = evi[0] % 2
                        mm_w = [(dwq[p % 2], pan_dg[p]),
                                (dpre, pre_all), (act, psA_free[bank])]
                        for dc in range(DC):
                            E("t", (lambda e, dc=dc, bank=bank, t0=t0, tn=tn, oc=oc, buf=buf:
                                    e.matmul(pp[bank][:, 0:tn],
                                             buf[:, dc * PA + oc * 128: dc * PA + oc * 128 + 128],
                                             s_xbf[:, dc * T + t0: dc * T + t0 + tn],
                                             start=(dc == 0), stop=(dc == DC - 1))),
                              waits=(mm_w + [(dx, dx_all)]) if dc == 0 else [],
                              inc=pe if dc == DC - 1 else None)
                        pe_mm = pe.n
                        if is_q:
                            par = evi[0] % 2
                            qb = qev_bufs[par]
                            E("a", (lambda e, bank=bank, tn=tn, qb=qb, o_g=o_g:
                                    e.activation(qb[:, 0:tn], pp[bank][:, 0:tn], AF.Identity,
                                                 bias=s_bq[:, o_g // 128: o_g // 128 + 1])),
                              waits=[(pe, pe_mm), (dqs[par], qev_free[par])], inc=act)
                            psA_free[bank] = act.n
                            DMA("s", dqs[par], q_scr[o_g:o_g + 128, t0:t0 + tn],
                                qb[:, 0:tn], waits=[(act, act.n)])
                            qev_free[par] = dqs[par].n
                        else:
                            hk = (o_g - D) // 128
                            E("a", (lambda e, bank=bank, t0=t0, tn=tn, hk=hk, o_g=o_g:
                                    e.activation(s_K[:, hk * T + t0: hk * T + t0 + tn],
                                                 pp[bank][:, 0:tn], AF.Identity,
                                                 bias=s_bq[:, o_g // 128: o_g // 128 + 1])),
                              waits=[(pe, pe_mm)], inc=act)
                            psA_free[bank] = act.n
                        evi[0] += 1
                else:
                    hv0 = o_g - 2 * D
                    for tc in range(TC):
                        bank = evi[0] % 2
                        mm_w = [(dwq[p % 2], pan_dg[p]),
                                (act, psA_free[bank])]
                        for dc in range(DC):
                            E("t", (lambda e, dc=dc, bank=bank, tc=tc, oc=oc, buf=buf:
                                    e.matmul(pp[bank][:, 0:128],
                                             s_xbf[:, dc * T + tc * 128: dc * T + tc * 128 + 128],
                                             buf[:, dc * PA + oc * 128: dc * PA + oc * 128 + 128],
                                             start=(dc == 0), stop=(dc == DC - 1))),
                              waits=(mm_w + [(dx, dx_all)]) if dc == 0 else [],
                              inc=pe if dc == DC - 1 else None)
                        pe_mm = pe.n
                        par = evi[0] % 2
                        qb = qev_bufs[par]
                        E("a", (lambda e, bank=bank, qb=qb:
                                e.activation(qb[:, 0:128], pp[bank][:, 0:128], AF.Copy)),
                          waits=[(pe, pe_mm), (dqs[par], qev_free[par])], inc=act)
                        psA_free[bank] = act.n
                        DMA("s", dqs[par], v_scr[tc * 128:(tc + 1) * 128, hv0:hv0 + 128],
                            qb[:, 0:128], waits=[(act, act.n)])
                        qev_free[par] = dqs[par].n
                        evi[0] += 1
            pan_pe_done[p] = pe.n
            if p + 2 < NPAN:
                load_panel(p + 2)

        endA = dict(pe=pe.n, act=act.n, dve=dve.n,
                    dqs0=dqs[0].n, dqs1=dqs[1].n)

        probsT_v = s_probsT[:, :].rearrange("p (c q) -> p c q", q=QBS)
        vp_v = [b[:, :].rearrange("p (c o) -> p c o", o=256) for b in vp_bufs]
        wo_v = [b[:, :].rearrange("p (c o) -> p c o", o=256) for b in wo_bufs]
        vscr_v = v_scr[:, :].rearrange("(c p) h -> p c h", p=128)
        qscr_v = q_scr[:, :].rearrange("(c p) t -> p c t", p=128)
        woutT_v = woutT[:, :].rearrange("(c p) o -> p c o", p=128)
        xT_v = xT[:, :].rearrange("(c p) t -> p c t", p=128)

        first_B = [(pe, endA["pe"]), (act, endA["act"]), (dve, endA["dve"]),
                   (dqs[0], endA["dqs0"]), (dqs[1], endA["dqs1"])]

        spill_counts = {}
        q_bufs = (s_q0, s_q1)
        pending_ln = [None]
        vp_gate = {0: endA["pe"], 1: endA["pe"]}
        wo_gate = {0: endA["pe"], 1: endA["pe"]}
        for qblk in range(NQB):
            tok0 = qblk * QBS
            dve_probs_prev = 0
            dsn_q = 0
            dve_m_prev = 0
            act_exp_prev = 0
            pe_tr_done = [0]
            pe_scores_qt = {}
            pending_tr = []

            def emit_tr(qt, dve_nrm):
                for tc in range(TC):
                    tb_ = tc % 2
                    E("t", (lambda e, tc=tc, tb_=tb_:
                            e.transpose(p_tr[tb_][:, :], s_probs[:, tc * 128:(tc + 1) * 128],
                                        s_id[:, :])),
                      waits=([(dve, dve_nrm), (act, bank_free.get(("tr", tb_), 0))]
                             if tc == 0 else [(act, bank_free.get(("tr", tb_), 0))]),
                      inc=pe)
                    E("a", (lambda e, tc=tc, qt=qt, tb_=tb_:
                            e.activation(probsT_v[:, tc, qt * 128:qt * 128 + 128],
                                         p_tr[tb_][:, :], AF.Copy)),
                      waits=[(pe, pe.n)], inc=act)
                    bank_free[("tr", tb_)] = act.n
                pe_tr_done[0] = pe.n

            NQT_B = QBS // 128
            dqr_at = {}

            def load_q(qt):
                par = qt % 2
                if qt == 0:
                    w = first_B + [(pe, bank_free.get("sq", 0))]
                elif qt == 1:
                    w = [(pe, bank_free.get("sq", 0))]
                else:
                    w = [(pe, pe_scores_qt[qt - 2])]
                DMA("s", dqr[par],
                    q_bufs[par][:, :].rearrange("p (c t) -> p c t", t=QW),
                    qscr_v[:, :, tok0 + qt * QW: tok0 + (qt + 1) * QW], waits=w)
                dqr_at[qt] = dqr[par].n

            load_q(0)
            if NQT_B > 1:
                load_q(1)
            for qt in range(NQT_B):
                qpar = qt % 2
                first_B = []
                dsn_q = dqr_at[qt]
                for ki, (k0, kn) in enumerate(KTB):
                    skb = ki % 2
                    E_w = [(act, max(bank_free.get(("scb", skb), 0),
                                     bank_free.get(("ctx", skb), 0)))]
                    for hc in range(DC):
                        E("t", (lambda e, skb=skb, hc=hc, k0=k0, kn=kn, qpar=qpar:
                                e.matmul(pp[skb][:, 0:kn],
                                         q_bufs[qpar][:, hc * QW: hc * QW + 128],
                                         s_K[:, hc * T + k0: hc * T + k0 + kn],
                                         start=(hc == 0), stop=(hc == DC - 1))),
                          waits=(([(dqr[qpar], dsn_q)] + E_w) if hc == 0 else []),
                          inc=pe if hc == DC - 1 else None)
                    E("a", (lambda e, skb=skb, k0=k0, kn=kn:
                            e.activation(s_scores[:, k0:k0 + kn], pp[skb][:, 0:kn],
                                         AF.Copy)),
                      waits=[(pe, pe.n), (dve, dve_m_prev), (act, act_exp_prev)],
                      inc=act)
                    bank_free[("scb", skb)] = act.n
                pe_scores_qt[qt] = pe.n
                if qt + 2 < NQT_B:
                    load_q(qt + 2)
                if pending_tr:
                    emit_tr(*pending_tr.pop())
                if pending_ln[0] is not None:
                    if next(pending_ln[0], "DONE") == "DONE":
                        pending_ln[0] = None
                act_sc = act.n
                E("v", (lambda e:
                        e.reduce_max(s_negm[:, 0:1], s_scores[:, :], AX.X, negate=True)),
                  waits=[(act, act_sc)], inc=dve)
                E("v", lambda e: e.tensor_scalar_mul(s_negm[:, 1:2], s_negm[:, 0:1], RSQH),
                  waits=[(dve, dve.n)], inc=dve)
                dve_m = dve.n
                dve_m_prev = dve_m
                E("a", (lambda e:
                        e.activation(s_probs[:, :], s_scores[:, :],
                                     AF.Exp, bias=s_negm[:, 1:2], scale=RSQH)),
                  waits=[(dve, dve_m), (pe, pe_tr_done[0])], inc=act)
                act_exp = act.n
                act_exp_prev = act_exp
                E("v", lambda e: e.reduce_sum(s_negm[:, 2:3], s_probs[:, :], AX.X),
                  waits=[(act, act_exp)] + ([(dqr, dqr.n)] if debug_outputs else []),
                  inc=dve)
                E("v", lambda e: e.reciprocal(s_negm[:, 3:4], s_negm[:, 2:3]),
                  waits=[(dve, dve.n)], inc=dve)
                E("v", lambda e: e.tensor_scalar_mul(s_probs[:, :], s_probs[:, :],
                                                     s_negm[:, 3:4]),
                  waits=[(dve, dve.n)], inc=dve)
                dve_nrm = dve.n
                if debug_outputs:
                    DMA("s", dqr, pb_scr[tok0 + qt * 128: tok0 + qt * 128 + 128, :],
                        s_probs[:, :], waits=[(dve, dve_nrm)])
                pending_tr.append((qt, dve_nrm))
            if pending_tr:
                emit_tr(*pending_tr.pop())
            act_probsT = act.n

            def load_vp(hcp):
                par = hcp % 2
                DMA("s", dvp[par], vp_v[par][:, :, :],
                    vscr_v[:, :, hcp * 256: hcp * 256 + 256],
                    waits=[(pe, vp_gate[par])])
                dsn_vps[hcp] = dvp[par].n

            dsn_vps = {}
            load_vp(0)
            if DC // 2 > 1:
                load_vp(1)

            if pending_ln[0] is not None:
                while next(pending_ln[0], "DONE") != "DONE":
                    pass
                pending_ln[0] = None
                bank_free["xlnfree"] = dve.n
            if qblk == NQB - 1 and NQB > 1:
                w0 = [(pe, pe.n), (dpre, pre_all),
                      (dso[0], dso[0].n), (dso[1], dso[1].n)]
                for dc in range(DC):
                    DMA("g", dacc, s_acc[:, dc * TO: dc * TO + TBS[0][1]],
                        x1_scr[dc * 128:(dc + 1) * 128, 0:TBS[0][1]], waits=w0)
                    w0 = []
                dacc_h0 = dacc.n

            for hcp in range(DC // 2):
                vpar = hcp % 2
                dsn_vp = dsn_vps[hcp]
                for hh in range(2):
                    hc = hcp * 2 + hh
                    bank = hc % 2
                    for tc in range(TC):
                        E("t", (lambda e, tc=tc, hh=hh, bank=bank, vpar=vpar:
                                e.matmul(pp[bank][:, 0:QBS],
                                         vp_v[vpar][:, tc, hh * 128: hh * 128 + 128],
                                         probsT_v[:, tc, :],
                                         start=(tc == 0), stop=(tc == TC - 1))),
                          waits=([(dvp[vpar], dsn_vp),
                                  (act, max(act_probsT, bank_free.get(("ctx", bank), 0),
                                            bank_free.get(("scb", bank), 0)))]
                                 if tc == 0 else []),
                          inc=pe if tc == TC - 1 else None)
                    E("a", (lambda e, hc=hc, bank=bank:
                            e.activation(s_ctx[:, hc * QBS:(hc + 1) * QBS],
                                         pp[bank][:, 0:QBS], AF.Identity,
                                         bias=s_bq[:, 2 * DC + hc: 2 * DC + hc + 1])),
                      waits=[(pe, pe.n)], inc=act)
                    bank_free[("ctx", bank)] = act.n
                    if debug_outputs:
                        DMA("s", dvp[vpar], ctx_scr[hc * 128:(hc + 1) * 128, tok0:tok0 + QBS],
                            s_ctx[:, hc * QBS:(hc + 1) * QBS], waits=[(act, act.n)])
                vp_gate[vpar] = pe.n
                if hcp + 2 < DC // 2:
                    load_vp(hcp + 2)
            act_ctx = act.n

            dg_wos = {}

            def load_wo(opan):
                par = opan % 2
                DMA("g", dwo[par], wo_v[par][:, :, :],
                    woutT_v[:, :, opan * 256: opan * 256 + 256],
                    waits=[(pe, wo_gate[par])])
                dg_wos[opan] = dwo[par].n

            load_wo(0)
            if DC // 2 > 1:
                load_wo(1)
            for opan in range(DC // 2):
                wpar = opan % 2
                dg_wo = dg_wos[opan]
                for oh in range(2):
                    oc = opan * 2 + oh
                    par = oc % 2
                    xow = xow_bufs[par]
                    DMA("s", dxw[par], xow[:, 0:QBS], xT_v[:, oc, tok0:tok0 + QBS],
                        waits=[(dve, bank_free.get(("xow", par), 0))])
                    dsn_x = dxw[par].n
                    bank = 2 + (oc % 2)
                    for hc in range(DC):
                        E("t", (lambda e, hc=hc, oh=oh, bank=bank, wpar=wpar:
                                e.matmul(pp[bank][:, 0:QBS],
                                         wo_v[wpar][:, hc, oh * 128: oh * 128 + 128],
                                         s_ctx[:, hc * QBS:(hc + 1) * QBS],
                                         start=(hc == 0), stop=(hc == DC - 1))),
                          waits=([(dwo[wpar], dg_wo),
                                  (act, max(act_ctx, bank_free.get(("wo", bank), 0))),
                                  (dve, bank_free.get(("lnbc", 0), 0))]
                                 if hc == 0 else []),
                          inc=pe if hc == DC - 1 else None)
                    E("a", (lambda e, oc=oc, bank=bank:
                            e.activation(s_xln[:, oc * QBS:(oc + 1) * QBS],
                                         pp[bank][:, 0:QBS], AF.Identity,
                                         bias=s_bo[:, oc:oc + 1])),
                      waits=[(pe, pe.n), (dve, bank_free.get("xlnfree", 0)),
                             (po, bank_free.get("xlnfree_po", 0))], inc=act)
                    bank_free[("wo", bank)] = act.n
                    E("v", (lambda e, oc=oc, xow=xow:
                            e.tensor_add(s_xln[:, oc * QBS:(oc + 1) * QBS],
                                         s_xln[:, oc * QBS:(oc + 1) * QBS],
                                         xow[:, 0:QBS])),
                      waits=[(act, act.n), (dxw[par], dsn_x)], inc=dve)
                    bank_free[("xow", par)] = dve.n
                    if debug_outputs:
                        DMA("s", dxw[par], h1_scr[oc * 128:(oc + 1) * 128, tok0:tok0 + QBS],
                            s_xln[:, oc * QBS:(oc + 1) * QBS], waits=[(dve, dve.n)])
                wo_gate[wpar] = pe.n
                if opan + 2 < DC // 2:
                    load_wo(opan + 2)
            dve_h1 = dve.n
            if qblk == NQB - 1:
                pe_wo_end = pe.n
                dve_h1_end = dve_h1

            if qblk == NQB - 1:
                assert pending_ln[0] is None
                ln1_apply_dve = run_ln(s_xln, 0, QBS, dve_h1, None, tok0, QBS,
                                       acc_dst=s_acc, acc_stride=TO)
            else:
                pending_ln[0] = layernorm_gen(s_xln, 0, QBS, dve_h1, x1_scr,
                                              tok0, QBS)
                next(pending_ln[0], None)
            bank_free["xlnfree"] = dve.n
            bank_free["xlnfree_po"] = po.n
            bank_free["sq"] = pe.n

        endB = dict(pe=pe.n, act=act.n, dve=dve.n, po=po.n,
                    dso0=dso[0].n, dso1=dso[1].n)

        dsn_acc = dacc.n
        first_x = [(pe, pe_wo_end), (act, endB["act"])]
        act_xlnb_dc = {}
        last_t0 = TBS[-1][0] if NQB > 1 else None
        for (t0r, tnr) in TBS:
            for dc in range(DC):
                w = list(first_x)
                first_x = []
                if t0r == last_t0 or NQB == 1:
                    w.append((dve, ln1_apply_dve[dc]))
                else:
                    w.append((dacc, dsn_acc))
                E("a", (lambda e, dc=dc, t0r=t0r, tnr=tnr:
                        e.activation(s_xlnb[:, dc * TO + t0r: dc * TO + t0r + tnr],
                                     s_acc[:, dc * TO + t0r: dc * TO + t0r + tnr], AF.Copy)),
                  waits=w, inc=act)
                act_xlnb_dc[(t0r, dc)] = act.n
        act_xlnb = act.n

        aup_v = s_aup[:, :].rearrange("p (c r) -> p c r", r=R)
        adn_v = s_adn[:, :].rearrange("p (c r) -> p c r", r=R)

        t1_emitted = [False]
        dve_t1_cell = [0]

        def emit_t1():
            for bi, (t0, tn) in enumerate(TBS):
                for dc in range(DC):
                    E("t", (lambda e, dc=dc, bi=bi, t0=t0, tn=tn:
                            e.matmul(pst[bi][0:R, 0:tn], aup_v[:, dc, :],
                                     s_xlnb[:, dc * TO + t0: dc * TO + t0 + tn],
                                     start=(dc == 0), stop=(dc == DC - 1),
                                     skip_group_check=True)),
                      waits=([(act, act_xlnb_dc[(t0, dc)])] +
                             ([(dve, bank_free.get("lnrows", 0))] if dc == 0 else [])),
                      inc=pe if dc == DC - 1 else None)
                E("v", (lambda e, bi=bi, t0=t0, tn=tn:
                        e.tensor_copy(s_t1[0:R, t0:t0 + tn], pst[bi][0:R, 0:tn])),
                  waits=[(pe, pe.n)], inc=dve)
            dve_t1_cell[0] = dve.n
            t1_emitted[0] = True

        uq_v = [b[:, :].rearrange("p (c o) -> p c o", o=128) for b in uq_bufs]
        dq_v = [b[:, :].rearrange("p (c o) -> p c o", o=128) for b in dq_bufs]
        wupq_v = wupqT[:, :].rearrange("(c p) h -> p c h", p=128)
        sexpu_v = sexpu[:, :].rearrange("(c p) h -> p c h", p=128)
        dnq_v = wdnqT[:, :].rearrange("(c p) d -> p c d", p=128)
        sexpd_v = sexpd[:, :].rearrange("(c p) d -> p c d", p=128)

        up_dg, up_dve, up_pe_done = {}, {}, {}
        dn_dg, dn_dve, dn_pe_done = {}, {}, {}

        NBUF = 3

        def load_up(u):
            w = ([(pe, up_pe_done[u - NBUF])] if u - NBUF in up_pe_done else
                 [(pe, pe_wo_end), (dve, dve_h1_end)])
            DMA("g", duq[u % NBUF], uq_v[u % NBUF][:, :, :],
                wupq_v[:, :, u * 128:(u + 1) * 128], waits=w)
            DMA("g", dus[u % NBUF],
                us_bufs[u % NBUF][:, :].rearrange("p (c o) -> p c o", o=128),
                sexpu_v[:, :, u * 128:(u + 1) * 128])
            up_dg[u] = (duq[u % NBUF].n, dus[u % NBUF].n)

        def deq_up(u, eng="v"):
            esem = dve if eng == "v" else po
            if eng == "v":
                E(eng, (lambda e, u=u:
                        e.scalar_tensor_tensor(uq_bufs[u % NBUF][:, :], uq_bufs[u % NBUF][:, :],
                                               7.5, us_bufs[u % NBUF][:, :],
                                               ALU.subtract, ALU.mult)),
                  waits=[(duq[u % NBUF], up_dg[u][0]), (dus[u % NBUF], up_dg[u][1])],
                  inc=esem)
            else:
                E(eng, (lambda e, u=u:
                        e.tensor_scalar_add(uq_bufs[u % NBUF][:, :],
                                            uq_bufs[u % NBUF][:, :], -7.5)),
                  waits=[(duq[u % NBUF], up_dg[u][0])], inc=esem)
                E(eng, (lambda e, u=u:
                        e.tensor_mul(uq_bufs[u % NBUF][:, :], uq_bufs[u % NBUF][:, :],
                                     us_bufs[u % NBUF][:, :])),
                  waits=[(dus[u % NBUF], up_dg[u][1]), (esem, esem.n)], inc=esem)
            up_dve[u] = (esem, esem.n)

        def load_dn(gi, dc):
            key = gi * DC + dc
            w = ([(pe, dn_pe_done[key - NBUF])] if key - NBUF in dn_pe_done else
                 [(pe, endB["pe"]), (act, endB["act"]), (dve, endB["dve"]),
                  (po, endB["po"])])
            DMA("g", ddn[key % NBUF], dq_v[key % NBUF][:, :, :],
                dnq_v[:, gi * GS:(gi + 1) * GS, dc * 128:(dc + 1) * 128], waits=w)
            DMA("g", dds[key % NBUF],
                ds_bufs[key % NBUF][:, :].rearrange("p (c o) -> p c o", o=128),
                sexpd_v[:, gi * GS:(gi + 1) * GS, dc * 128:(dc + 1) * 128])
            dn_dg[key] = (ddn[key % NBUF].n, dds[key % NBUF].n)

        def deq_dn(key):
            E("v", (lambda e, key=key:
                    e.scalar_tensor_tensor(dq_bufs[key % NBUF][:, :], dq_bufs[key % NBUF][:, :],
                                           7.5, ds_bufs[key % NBUF][:, :],
                                           ALU.subtract, ALU.mult)),
              waits=[(ddn[key % NBUF], dn_dg[key][0]), (dds[key % NBUF], dn_dg[key][1])],
              inc=dve)
            dn_dve[key] = dve.n

        for u0 in range(min(NBUF, GS * NG)):
            load_up(u0)

        E("a", lambda e: e.memzero(s_bupl[:, :]),
          waits=[(pe, endB["pe"]), (act, endB["act"]), (dve, endB["dve"]),
                 (po, endB["po"])], inc=act)
        E("a", lambda e: e.memzero(s_bdnl[:, :]), waits=[(act, act.n)], inc=act)
        act_lbz = act.n
        DMA("g", dlb, s_bupl[0:R, :], bupl[:, :],
            waits=[(act, act_lbz), (dpre, pre_all)])
        DMA("g", dlb, s_bdnl[0:R, :], bdnl[:, :])
        dlb_all = dlb.n


        for gi in range(NG):
            for i in range(GS):
                u = gi * GS + i
                if u not in up_dve:
                    deq_up(u)
                if u + 1 < GS * NG and (u + 1) not in up_dve:
                    deq_up(u + 1)
                for bi, (t0, tn) in enumerate(TBS):
                    bank = (len(TBS) * u + bi) % 4
                    for dc in range(DC):
                        w_mm = []
                        if dc == 0:
                            w_mm = [up_dve[u],
                                    (dve, bank_free.get(("ppV", bank), 0)),
                                    (act, max(bank_free.get(("ppA", bank), 0),
                                              endB["act"] if u < 2 else 0))]
                        if not t1_emitted[0]:
                            w_mm = w_mm + [(act, act_xlnb_dc[(t0, dc)])]
                        E("t", (lambda e, dc=dc, u=u, bank=bank, t0=t0, tn=tn:
                                e.matmul(pp[bank][:, 0:tn], uq_v[u % NBUF][:, dc, :],
                                         s_xlnb[:, dc * TO + t0: dc * TO + t0 + tn],
                                         start=(dc == 0), stop=False)),
                          waits=w_mm)
                    if not t1_emitted[0]:
                        emit_t1()
                    E("t", (lambda e, u=u, bank=bank, t0=t0, tn=tn:
                            e.matmul(pp[bank][:, 0:tn],
                                     s_bupl[:, u * 128:(u + 1) * 128],
                                     s_t1[:, t0:t0 + tn], start=False, stop=True)),
                      waits=([(dlb, dlb_all), (dve, dve_t1_cell[0])]
                             if u == 0 else []), inc=pe)
                    E("a", (lambda e, i=i, u=u, bank=bank, t0=t0, tn=tn:
                            e.activation(s_x2[:, i * TO + t0: i * TO + t0 + tn],
                                         pp[bank][:, 0:tn], AF.Relu,
                                         bias=s_bu[:, u:u + 1])),
                      waits=[(pe, pe.n)], inc=act)
                    bank_free[("ppA", bank)] = act.n
                up_pe_done[u] = pe.n
                if u + NBUF < GS * NG:
                    load_up(u + NBUF)
            act_x2 = act.n

            for bi, (t0, tn) in enumerate(TBS):
                for i in range(GS):
                    hcg = gi * GS + i
                    E("t", (lambda e, i=i, hcg=hcg, bi=bi, t0=t0, tn=tn, gi=gi:
                            e.matmul(pst[bi][0:R, 0:tn], adn_v[:, hcg, :],
                                     s_x2[:, i * TO + t0: i * TO + t0 + tn],
                                     start=(gi == 0 and i == 0),
                                     stop=(gi == NG - 1 and i == GS - 1),
                                     skip_group_check=True)),
                      waits=([(act, act_x2), (dve, dve_t1_cell[0])] if i == 0 else []),
                      inc=pe if (gi == NG - 1 and i == GS - 1 and bi == len(TBS) - 1) else None)

            if gi == 0:
                for d0 in range(min(NBUF, DC)):
                    load_dn(0, d0)
            for dc in range(DC):
                key = gi * DC + dc
                if dc == 0 and gi == 0:
                    deq_dn(key)
                for k2 in (key + 1, key + 2):
                    if k2 < NG * DC and k2 <= key + NBUF - 1 and                        k2 > max(dn_dve, default=-1):
                        deq_dn(k2)
                for bi, (t0, tn) in enumerate(TBS):
                    bank = (len(TBS) * dc + bi) % 4
                    for i in range(GS):
                        E("t", (lambda e, i=i, key=key, bank=bank, t0=t0, tn=tn:
                                e.matmul(pp[bank][:, 0:tn], dq_v[key % NBUF][:, i, :],
                                         s_x2[:, i * TO + t0: i * TO + t0 + tn],
                                         start=(i == 0), stop=(i == GS - 1))),
                          waits=([(act, max(act_x2, bank_free.get(("ppA", bank), 0))),
                                  (dve, max(dn_dve[key],
                                            bank_free.get(("ppV", bank), 0)))]
                                 if i == 0 else []),
                          inc=pe if i == GS - 1 else None)
                    E("v", (lambda e, dc=dc, bank=bank, t0=t0, tn=tn:
                            e.tensor_add(s_acc[:, dc * TO + t0: dc * TO + t0 + tn],
                                         pp[bank][:, 0:tn],
                                         s_acc[:, dc * TO + t0: dc * TO + t0 + tn])),
                      waits=[(pe, pe.n), (dacc, dsn_acc)], inc=dve)
                    bank_free[("ppV", bank)] = dve.n
                dn_pe_done[key] = pe.n
                if dc + NBUF < DC:
                    load_dn(gi, dc + NBUF)
                elif gi + 1 < NG and dc + NBUF - DC < min(NBUF, DC):
                    load_dn(gi + 1, dc + NBUF - DC)

        acc_done = {}
        for bi, (t0, tn) in enumerate(TBS):
            E("v", (lambda e, bi=bi, t0=t0, tn=tn:
                    e.tensor_copy(s_t2[0:R, t0:t0 + tn], pst[bi][0:R, 0:tn])),
              waits=[(pe, pe.n)], inc=dve)
        dve_t2 = dve.n
        for dc in range(DC):
            for bi, (t0, tn) in enumerate(TBS):
                bank = (len(TBS) * dc + bi) % 4
                E("t", (lambda e, dc=dc, bank=bank, t0=t0, tn=tn:
                        e.matmul(pp[bank][:, 0:tn], s_bdnl[:, dc * 128:(dc + 1) * 128],
                                 s_t2[:, t0:t0 + tn], start=True, stop=True)),
                  waits=[(dve, max(dve_t2, bank_free.get(("ppV", bank), 0),
                                   bank_free.get(("lb", bank), 0))),
                         (act, bank_free.get(("ppA", bank), 0)),
                         (dlb, dlb_all)], inc=pe)
                E("v", (lambda e, dc=dc, bank=bank, t0=t0, tn=tn:
                        e.scalar_tensor_tensor(
                            s_acc[:, dc * TO + t0: dc * TO + t0 + tn],
                            pp[bank][:, 0:tn], s_bd[:, dc:dc + 1],
                            s_acc[:, dc * TO + t0: dc * TO + t0 + tn],
                            ALU.add, ALU.add)),
                  waits=[(pe, pe.n)], inc=dve)
                bank_free[("lb", bank)] = dve.n
                bank_free[("ppV", bank)] = dve.n
            acc_done[dc] = (dve, dve.n)
        dve_acc = dve.n

        for (t0, tn) in TBS:
            run_ln(s_acc, t0, tn, dve_acc, y, t0, TO, ready_map=acc_done)

        E("s", None, waits=[(dso[0], dso[0].n), (dso[1], dso[1].n)])

        def replay(key):
            def f(eng):
                for fn, waits, incs in ops[key]:
                    for h, v in waits:
                        eng.wait_ge(h, v)
                    if fn is not None:
                        ins = fn(eng)
                        for h, d in incs:
                            ins.then_inc(h, d)
            return f

        with nc.Block() as blk:
            blk.gpsimd(replay("g"))
            blk.sync(replay("s"))
            blk.tensor(replay("t"))
            blk.vector(replay("v"))
            blk.scalar(replay("a"))

    return nc


def make_core_inputs(cfg, x_b, w_qkv, b_qkv, w_out, b_out, w_up_q, w_up_scale,
                     b_up, a_up, b_upl, w_dn_q, w_dn_scale, b_dn, a_dn, b_dnl,
                     half, shared):
    D, T, TO, FF, R = cfg["D"], cfg["T"], cfg["TO"], cfg["FF"], cfg["R"]
    f32 = np.float32
    xTb = np.ascontiguousarray(x_b.T.astype(f32, copy=False))
    if half == 1:
        xTb = np.ascontiguousarray(
            np.concatenate([xTb[:, TO:], xTb[:, :TO]], axis=1))
    ins = {"xT": xTb}
    ins.update(shared)
    return ins


def make_shared_inputs(cfg, w_qkv, b_qkv, w_out, b_out, w_up_q, w_up_scale,
                       b_up, a_up, b_upl, w_dn_q, w_dn_scale, b_dn, a_dn,
                       b_dnl):
    D, FF = cfg["D"], cfg["FF"]
    BLK = 64
    f32 = np.float32

    def pp_(v, n):
        return np.ascontiguousarray(np.asarray(v, dtype=f32).reshape(n, 128).T)

    return {
        "wqkvT": np.ascontiguousarray(np.asarray(w_qkv, dtype=f32).T),
        "woutT": np.ascontiguousarray(np.asarray(w_out, dtype=f32).T),
        "bqkv_pp": pp_(b_qkv, 3 * D // 128),
        "bout_pp": pp_(b_out, D // 128),
        "wupqT": np.ascontiguousarray(np.asarray(w_up_q, dtype=np.int32).T),
        "sexpu": np.ascontiguousarray(
            np.repeat(np.asarray(w_up_scale, dtype=f32).T, BLK, axis=0)),
        "bup_pp": pp_(b_up, FF // 128),
        "aup": np.ascontiguousarray(np.asarray(a_up, dtype=f32)),
        "bupl": np.ascontiguousarray(np.asarray(b_upl, dtype=f32)),
        "wdnqT": np.ascontiguousarray(np.asarray(w_dn_q, dtype=np.int32).T),
        "sexpd": np.ascontiguousarray(
            np.repeat(np.asarray(w_dn_scale, dtype=f32).T, BLK, axis=0)),
        "bdn_pp": pp_(b_dn, D // 128),
        "adn": np.ascontiguousarray(np.asarray(a_dn, dtype=f32)),
        "bdnl": np.ascontiguousarray(np.asarray(b_dnl, dtype=f32)),
        "ident": np.eye(128, dtype=f32),
    }


def kernel(L0_attn_hidden_states, L0_attn_w_qkv, L0_attn_b_qkv, L0_attn_w_out,
           L0_attn_b_out, L0_ffn_w_up_q, L0_ffn_w_up_scale, L0_ffn_b_up,
           L0_ffn_w_up_lora_a, L0_ffn_w_up_lora_b, L0_ffn_w_down_q,
           L0_ffn_w_down_scale, L0_ffn_b_down, L0_ffn_w_down_lora_a,
           L0_ffn_w_down_lora_b):
    cfg = CFG_FULL
    D, T, TO = cfg["D"], cfg["T"], cfg["TO"]
    x = np.asarray(L0_attn_hidden_states, dtype=np.float32)
    B = x.shape[0]

    nc = build_program(cfg)
    shared = make_shared_inputs(
        cfg, L0_attn_w_qkv, L0_attn_b_qkv, L0_attn_w_out, L0_attn_b_out,
        L0_ffn_w_up_q, L0_ffn_w_up_scale, L0_ffn_b_up, L0_ffn_w_up_lora_a,
        L0_ffn_w_up_lora_b, L0_ffn_w_down_q, L0_ffn_w_down_scale,
        L0_ffn_b_down, L0_ffn_w_down_lora_a, L0_ffn_w_down_lora_b)

    in_maps = []
    for core in range(8):
        b, half = core // 2, core % 2
        in_maps.append(make_core_inputs(
            cfg, x[b], None, None, None, None, None, None, None, None, None,
            None, None, None, None, None, half, shared))

    res = run_bass_kernel_spmd(nc, in_maps, core_ids=list(range(8)))

    out = np.empty((B, T, D), dtype=np.float32)
    for core in range(8):
        b, half = core // 2, core % 2
        out[b, half * TO:(half + 1) * TO, :] = res.results[core]["y"].T
    return out


# version 21
# speedup vs baseline: 1.1021x; 1.1021x over previous
import contextlib
import numpy as np
import concourse.bass as bass
import concourse.mybir as mybir
from concourse.bass_utils import run_bass_kernel_spmd

F32 = mybir.dt.float32
BF16 = mybir.dt.bfloat16
I32 = mybir.dt.int32
AF = mybir.ActivationFunctionType
ALU = mybir.AluOpType
AX = mybir.AxisListType

EPS = 1e-5
CFG_FULL = dict(D=2048, T=2048, TO=1024, FF=8192, R=16)


class Sem:
    def __init__(self, handle):
        self.h = handle
        self.n = 0


def build_program(cfg, debug_outputs=False):
    nc = bass.Bass("TRN2", target_bir_lowering=False)
    scratch_kind = "ExternalOutput" if debug_outputs else "Internal"
    D, T, TO, FF, R = cfg["D"], cfg["T"], cfg["TO"], cfg["FF"], cfg["R"]
    DC, FC, TC = D // 128, FF // 128, T // 128
    RSQH = float(1.0 / np.sqrt(D))

    def blocks(n):
        out, i = [], 0
        while i < n:
            out.append((i, min(512, n - i)))
            i += 512
        return out

    TBS = blocks(TO)
    KTB = blocks(T)
    QBS = min(512, TO)
    NQB = TO // QBS
    GS = min(8, FC)
    NG = FC // GS
    PA = min(256, D)
    NPAN = 3 * D // PA

    dram = nc.dram_tensor
    xT = dram("xT", [D, T], F32, kind="ExternalInput")
    wqkvT = dram("wqkvT", [D, 3 * D], F32, kind="ExternalInput")
    woutT = dram("woutT", [D, D], F32, kind="ExternalInput")
    bqkv_pp = dram("bqkv_pp", [128, 3 * DC], F32, kind="ExternalInput")
    bout_pp = dram("bout_pp", [128, DC], F32, kind="ExternalInput")
    wupqT = dram("wupqT", [D, FF], I32, kind="ExternalInput")
    sexpu = dram("sexpu", [D, FF], F32, kind="ExternalInput")
    bup_pp = dram("bup_pp", [128, FC], F32, kind="ExternalInput")
    aup = dram("aup", [D, R], F32, kind="ExternalInput")
    bupl = dram("bupl", [R, FF], F32, kind="ExternalInput")
    wdnqT = dram("wdnqT", [FF, D], I32, kind="ExternalInput")
    sexpd = dram("sexpd", [FF, D], F32, kind="ExternalInput")
    bdn_pp = dram("bdn_pp", [128, DC], F32, kind="ExternalInput")
    adn = dram("adn", [FF, R], F32, kind="ExternalInput")
    bdnl = dram("bdnl", [R, D], F32, kind="ExternalInput")
    ident = dram("ident", [128, 128], F32, kind="ExternalInput")
    y = dram("y", [D, TO], F32, kind="ExternalOutput")
    q_scr = dram("q_scr", [D, TO], BF16, kind=scratch_kind)
    if debug_outputs:
        ctx_scr = dram("ctx_scr", [D, TO], BF16, kind="ExternalOutput")
        h1_scr = dram("h1_scr", [D, TO], F32, kind="ExternalOutput")
        pb_scr = dram("pb_scr", [TO, T], BF16, kind="ExternalOutput")
    v_scr = dram("v_scr", [T, D], BF16, kind=scratch_kind)
    x1_scr = dram("x1_scr", [D, TO], F32, kind=scratch_kind)

    LIMIT = 211200
    ARENA = nc.bump_sbuf(LIMIT)[0]
    bump = [0]

    def sb(name, cols, dtype, at=None):
        esz = 2 if dtype == BF16 else 4
        if at is None:
            at = bump[0]
            bump[0] = at + ((cols * esz + 31) // 32) * 32
            assert bump[0] <= LIMIT, f"{name} overflows sbuf ({bump[0]})"
            end = bump[0]
        else:
            end = at + ((cols * esz + 31) // 32) * 32
            assert end <= LIMIT, f"{name} overflows sbuf ({end})"
        t = nc.alloc_sbuf_tensor_at(name, [128, cols], dtype, offset=ARENA + at)
        return t, end

    def sbp(name, cols, dtype):
        t, _ = sb(name, cols, dtype)
        return t

    s_bq = sbp("s_bq", 3 * DC, F32)
    s_bo = sbp("s_bo", DC, F32)
    s_bu = sbp("s_bu", FC, F32)
    s_bd = sbp("s_bd", DC, F32)
    s_aup = sbp("s_aup", DC * R, BF16)
    s_adn = sbp("s_adn", FC * R, BF16)
    s_t1 = sbp("s_t1", TO, BF16)
    s_t2 = sbp("s_t2", TO, BF16)
    s_ones1 = sbp("s_ones1", 128, F32)
    s_onesb = sbp("s_onesb", 16, BF16)
    s_id = sbp("s_id", 128, BF16)
    s_rows = sbp("s_rows", 512 * 3, F32)
    s_h1b = sbp("s_h1b", 512 * 2, BF16)
    s_h1sq = sbp("s_h1sq", 512 * 2, BF16)
    s_tmp = sbp("s_tmp", 512, F32)
    s_negm = sbp("s_negm", 8, F32)
    s_oev = sbp("s_oev", 512 * 2, F32)
    s_bcast = sbp("s_bcast", 512 * 2, F32)

    P0 = ((bump[0] + 63) // 64) * 64

    s_K, pend = sb("s_K", DC * T, BF16, at=P0)
    PK = ((pend + 63) // 64) * 64

    s_xbf, a1 = sb("s_xbf", DC * T, BF16, at=PK)
    s_wq0, a2 = sb("s_wq0", DC * PA, BF16, at=a1)
    s_wq1, a3 = sb("s_wq1", DC * PA, BF16, at=a2)
    s_qev0, a4 = sb("s_qev0", 512, BF16, at=a3)
    s_qev1, a5 = sb("s_qev1", 512, BF16, at=a4)

    QW = 128
    s_ctx, b1 = sb("s_ctx", DC * QBS, BF16, at=PK)
    s_q0, _ = sb("s_q0", DC * QW, BF16, at=PK)
    s_q1, _q1e = sb("s_q1", DC * QW, BF16, at=PK + ((DC * QW * 2 + 31) // 32) * 32)
    s_probs, b2 = sb("s_probs", T, BF16, at=b1)
    s_probsT, b3 = sb("s_probsT", TC * QBS, BF16, at=b2)
    s_vp0, b4 = sb("s_vp0", TC * 256, BF16, at=b3)
    s_vp1, b5 = sb("s_vp1", TC * 256, BF16, at=b4)
    s_wo0, b5b = sb("s_wo0", DC * 256, BF16, at=b5)
    s_wo1, b6 = sb("s_wo1", DC * 256, BF16, at=b5b)
    s_xow0, b7 = sb("s_xow0", QBS, F32, at=b6)
    s_xow1, b8 = sb("s_xow1", QBS, F32, at=b7)
    s_xln, b9 = sb("s_xln", DC * QBS, F32, at=b8)
    s_scores, b10 = sb("s_scores", T, F32, at=b9)

    s_acc, c1 = sb("s_acc", DC * TO, F32, at=P0)
    s_xlnb, c2 = sb("s_xlnb", DC * TO, BF16, at=c1)
    s_x2, c3 = sb("s_x2", GS * TO, BF16, at=c2)
    s_uq0, c4 = sb("s_uq0", DC * 128, BF16, at=c3)
    s_uq1, c5 = sb("s_uq1", DC * 128, BF16, at=c4)
    s_uq2, c5b = sb("s_uq2", DC * 128, BF16, at=c5)
    s_us0, c6 = sb("s_us0", DC * 128, BF16, at=c5b)
    s_us1, c7 = sb("s_us1", DC * 128, BF16, at=c6)
    s_us2, c7b = sb("s_us2", DC * 128, BF16, at=c7)
    s_dq0, c8 = sb("s_dq0", GS * 128, BF16, at=c7b)
    s_dq1, c9 = sb("s_dq1", GS * 128, BF16, at=c8)
    s_dq2, c9b = sb("s_dq2", GS * 128, BF16, at=c9)
    s_ds0, c10 = sb("s_ds0", GS * 128, BF16, at=c9b)
    s_ds1, c11 = sb("s_ds1", GS * 128, BF16, at=c10)
    s_ds2, c11b = sb("s_ds2", GS * 128, BF16, at=c11)
    s_bupl, c12 = sb("s_bupl", FF, BF16, at=c11b)
    s_bdnl, c13 = sb("s_bdnl", D, BF16, at=c12)

    ops = {k: [] for k in ("t", "v", "a", "g", "s")}

    def E(eng, fn, waits=(), inc=None, inc_by=1):
        incs = []
        if inc is not None:
            inc.n += inc_by
            incs.append((inc.h, inc_by))
        ops[eng].append((fn, [(s.h, v) for s, v in waits if v > 0], incs))

    def DMA(eng, sem, dst, src, waits=()):
        E(eng, lambda e: e.dma_start(dst, src), waits, inc=sem, inc_by=16)

    with contextlib.ExitStack() as stack:
        en = stack.enter_context
        pp = [en(nc.psum_tensor(f"pp{i}", [128, 512], F32)) for i in range(4)]
        p_tr = [en(nc.psum_tensor("p_tr0", [128, 128], BF16)),
                en(nc.psum_tensor("p_tr1", [128, 128], BF16))]
        pst = [en(nc.psum_tensor(f"pst{i}", [16, 512], F32)) for i in range(2)]
        def S(name):
            return Sem(en(nc.semaphore(name)))
        pe, act, dve, po = S("pe"), S("act"), S("dve"), S("po")
        dpre, dx = S("dpre"), S("dx")
        dwq = [S("dwq0"), S("dwq1")]
        dqs = [S("dqs0"), S("dqs1")]
        dqr = [S("dqr0"), S("dqr1")]
        dvp = [S("dvp0"), S("dvp1")]
        dwo = [S("dwo0"), S("dwo1")]
        dxw = [S("dxw0"), S("dxw1")]
        dso = [S("dso0"), S("dso1")]
        dacc = S("dacc")
        duq = [S("duq0"), S("duq1"), S("duq2")]
        dus = [S("dus0"), S("dus1"), S("dus2")]
        ddn = [S("ddn0"), S("ddn1"), S("ddn2")]
        dds = [S("dds0"), S("dds1"), S("dds2")]
        dlb = S("dlb")

        wq_bufs = (s_wq0, s_wq1)
        qev_bufs = (s_qev0, s_qev1)
        xow_bufs = (s_xow0, s_xow1)
        uq_bufs, us_bufs = (s_uq0, s_uq1, s_uq2), (s_us0, s_us1, s_us2)
        dq_bufs, ds_bufs = (s_dq0, s_dq1, s_dq2), (s_ds0, s_ds1, s_ds2)
        vp_bufs = (s_vp0, s_vp1)
        wo_bufs = (s_wo0, s_wo1)

        bank_free = {}

        def run_ln(*args, **kw):
            res = {}
            for _ in layernorm_gen(*args, result=res, **kw):
                pass
            return res

        def layernorm_gen(src, tb0, tn, ready_dve, out_dram, out_col0, src_stride,
                          acc_dst=None, acc_stride=None, result=None,
                          ready_map=None):
            MEAN, VARC, STDC = 0, 512, 1024
            invD = 1.0 / D
            for dc in range(DC):
                par = dc % 2
                if ready_map is not None:
                    w = [ready_map[dc], (pe, bank_free.get(("lnst", par), 0))]
                else:
                    w = ([(dve, ready_dve)] if dc == 0 else []) + \
                        [(pe, bank_free.get(("lnst", par), 0))]
                E("a", (lambda e, dc=dc, par=par:
                        e.activation(s_h1b[:, par * 512: par * 512 + tn],
                                     src[:, dc * src_stride + tb0: dc * src_stride + tb0 + tn],
                                     AF.Copy)), waits=w, inc=act)
                E("a", (lambda e, dc=dc, par=par:
                        e.activation(s_h1sq[:, par * 512: par * 512 + tn],
                                     src[:, dc * src_stride + tb0: dc * src_stride + tb0 + tn],
                                     AF.Square)), inc=act)
                act_h = act.n
                E("t", (lambda e, dc=dc, par=par:
                        e.matmul(pst[0][0:1, 0:tn], s_onesb[:, 0:1],
                                 s_h1b[:, par * 512: par * 512 + tn],
                                 start=(dc == 0), stop=(dc == DC - 1),
                                 skip_group_check=True)),
                  waits=([(act, act_h)] +
                         ([(dve, bank_free.get("lnrows", 0))] if dc == 0 else [])))
                E("t", (lambda e, dc=dc, par=par:
                        e.matmul(pst[1][0:1, 0:tn], s_onesb[:, 0:1],
                                 s_h1sq[:, par * 512: par * 512 + tn],
                                 start=(dc == 0), stop=(dc == DC - 1),
                                 skip_group_check=True)), inc=pe)
                bank_free[("lnst", par)] = pe.n
                if dc % 4 == 3:
                    yield
            pe_stats = pe.n

            E("v", (lambda e: e.tensor_scalar_mul(
                s_rows[0:1, MEAN:MEAN + tn], pst[0][0:1, 0:tn], invD)),
              waits=[(pe, pe_stats)], inc=dve)
            E("v", (lambda e: e.tensor_scalar_mul(
                s_rows[0:1, VARC:VARC + tn], pst[1][0:1, 0:tn], invD)), inc=dve)
            E("a", (lambda e: e.activation(
                s_rows[0:1, STDC:STDC + tn], s_rows[0:1, MEAN:MEAN + tn], AF.Square)),
              waits=[(dve, dve.n)], inc=act)
            E("v", (lambda e: e.tensor_sub(
                s_rows[0:1, VARC:VARC + tn], s_rows[0:1, VARC:VARC + tn],
                s_rows[0:1, STDC:STDC + tn])), waits=[(act, act.n)], inc=dve)
            E("a", (lambda e: e.activation(
                s_rows[0:1, STDC:STDC + tn], s_rows[0:1, VARC:VARC + tn],
                AF.Sqrt, bias=s_negm[0:1, 7:8])), waits=[(dve, dve.n)], inc=act)
            E("v", (lambda e: e.reciprocal(
                s_rows[0:1, VARC:VARC + tn], s_rows[0:1, STDC:STDC + tn])),
              waits=[(act, act.n)], inc=dve)
            E("v", (lambda e: e.tensor_mul(
                s_rows[0:1, STDC:STDC + tn], s_rows[0:1, MEAN:MEAN + tn],
                s_rows[0:1, VARC:VARC + tn])), waits=[(dve, dve.n)], inc=dve)
            dve_rows = dve.n
            bank_free["lnrows"] = dve_rows

            yield
            E("t", (lambda e: e.matmul(pp[2][:, 0:tn], s_ones1[0:1, :],
                                       s_rows[0:1, VARC:VARC + tn],
                                       start=True, stop=True, skip_group_check=True)),
              waits=[(dve, max(dve_rows, bank_free.get(("lnbc", 0), 0),
                               bank_free.get(("ppV", 2), 0))),
                     (act, max(bank_free.get(("wo", 2), 0),
                               bank_free.get(("ppA", 2), 0)))])
            E("t", (lambda e: e.matmul(pp[3][:, 0:tn], s_ones1[0:1, :],
                                       s_rows[0:1, STDC:STDC + tn],
                                       start=True, stop=True, skip_group_check=True)),
              waits=[(dve, max(bank_free.get(("lnbc", 1), 0),
                               bank_free.get(("ppV", 3), 0))),
                     (act, bank_free.get(("ppA", 3), 0))], inc=pe)
            pe_bc = pe.n
            E("v", (lambda e: e.tensor_copy(s_bcast[:, 0:tn], pp[2][:, 0:tn])),
              waits=[(pe, pe_bc), (po, bank_free.get("ln_po", 0))], inc=dve)
            E("v", (lambda e: e.tensor_copy(s_bcast[:, 512:512 + tn], pp[3][:, 0:tn])),
              waits=[(dve, dve.n)], inc=dve)
            dve_bc = dve.n
            bank_free[("lnbc", 0)] = dve_bc
            bank_free[("lnbc", 1)] = dve_bc

            apply_dve = {}
            for dc in range(DC):
                par = dc % 2
                eng = "v"
                esem = dve
                if acc_dst is None:
                    dst = s_oev[:, par * 512: par * 512 + tn]
                else:
                    dst = acc_dst[:, dc * acc_stride + out_col0:
                                  dc * acc_stride + out_col0 + tn]
                w_mul = [(dve, dve_bc), (esem, esem.n)]
                if acc_dst is None:
                    w_mul.append((dso[par], bank_free.get(("lnout", par), 0)))
                E(eng, (lambda e, dc=dc, dst=dst:
                        e.tensor_mul(dst,
                                     src[:, dc * src_stride + tb0: dc * src_stride + tb0 + tn],
                                     s_bcast[:, 0:tn])),
                  waits=w_mul, inc=esem)
                E(eng, (lambda e, dst=dst:
                        e.tensor_sub(dst, dst, s_bcast[:, 512:512 + tn])),
                  waits=[(esem, esem.n)], inc=esem)
                if acc_dst is None:
                    DMA("s", dso[par],
                        out_dram[dc * 128:(dc + 1) * 128, out_col0:out_col0 + tn],
                        s_oev[:, par * 512: par * 512 + tn], waits=[(esem, esem.n)])
                    bank_free[("lnout", par)] = dso[par].n
                apply_dve[dc] = dve.n
            bank_free["ln_po"] = po.n
            if result is not None:
                result.update(apply_dve)

        E("v", lambda e: e.memset(s_t1[:, :], 0.0), inc=dve)
        E("v", lambda e: e.memset(s_t2[:, :], 0.0), inc=dve)
        E("v", lambda e: e.memset(s_ones1[:, :], 0.0), inc=dve)
        E("v", lambda e: e.memset(s_rows[:, :], 0.0), inc=dve)
        E("v", lambda e: e.memset(s_ones1[0:1, :], 1.0), waits=[(dve, dve.n)], inc=dve)
        E("v", lambda e: e.memset(s_onesb[:, 0:1], 1.0), inc=dve)
        E("v", lambda e: e.memset(s_negm[:, 7:8], EPS), inc=dve)
        pre_dve = dve.n

        DMA("g", dpre, s_bq[:, :], bqkv_pp[:, :])
        DMA("g", dpre, s_bo[:, :], bout_pp[:, :])
        DMA("g", dpre, s_bu[:, :], bup_pp[:, :])
        DMA("g", dpre, s_bd[:, :], bdn_pp[:, :])
        DMA("g", dpre, s_id[:, :], ident[:, :])
        DMA("g", dpre, s_aup[:, :].rearrange("p (c r) -> p c r", r=R),
            aup[:, :].rearrange("(c p) r -> p c r", p=128))
        DMA("g", dpre, s_adn[:, :].rearrange("p (c r) -> p c r", r=R),
            adn[:, :].rearrange("(c p) r -> p c r", p=128))
        pre_all = dpre.n

        dx_at = {}
        for dc in range(DC):
            DMA("g", dx, s_xbf[:, dc * T:(dc + 1) * T], xT[dc * 128:(dc + 1) * 128, :])
            dx_at[dc] = dx.n
        dx_all = dx.n

        wqv = [b[:, :].rearrange("p (c o) -> p c o", o=PA) for b in wq_bufs]
        wqkv_v = wqkvT[:, :].rearrange("(c p) o -> p c o", p=128)

        pan_dg, pan_pe_done = {}, {}
        psA_free = {0: 0, 1: 0}
        qev_free = {0: 0, 1: 0}
        evi = [0]

        def load_panel(p):
            w = [(pe, pan_pe_done[p - 2])] if p - 2 in pan_pe_done else []
            DMA("g", dwq[p % 2], wqv[p % 2][:, :, :],
                wqkv_v[:, :, p * PA:(p + 1) * PA], waits=w)
            pan_dg[p] = dwq[p % 2].n

        load_panel(0)
        if NPAN > 1:
            load_panel(1)

        for p in range(NPAN):
            buf = wq_bufs[p % 2]
            for oc in range(PA // 128):
                o_g = p * PA + oc * 128
                if o_g < 2 * D:
                    is_q = o_g < D
                    for (t0, tn) in (TBS if is_q else KTB):
                        bank = evi[0] % 2
                        mm_w = [(dwq[p % 2], pan_dg[p]),
                                (dpre, pre_all), (act, psA_free[bank])]
                        for dc in range(DC):
                            E("t", (lambda e, dc=dc, bank=bank, t0=t0, tn=tn, oc=oc, buf=buf:
                                    e.matmul(pp[bank][:, 0:tn],
                                             buf[:, dc * PA + oc * 128: dc * PA + oc * 128 + 128],
                                             s_xbf[:, dc * T + t0: dc * T + t0 + tn],
                                             start=(dc == 0), stop=(dc == DC - 1))),
                              waits=(mm_w + [(dx, dx_all)]) if dc == 0 else [],
                              inc=pe if dc == DC - 1 else None)
                        pe_mm = pe.n
                        if is_q:
                            par = evi[0] % 2
                            qb = qev_bufs[par]
                            E("a", (lambda e, bank=bank, tn=tn, qb=qb, o_g=o_g:
                                    e.activation(qb[:, 0:tn], pp[bank][:, 0:tn], AF.Identity,
                                                 bias=s_bq[:, o_g // 128: o_g // 128 + 1])),
                              waits=[(pe, pe_mm), (dqs[par], qev_free[par])], inc=act)
                            psA_free[bank] = act.n
                            DMA("s", dqs[par], q_scr[o_g:o_g + 128, t0:t0 + tn],
                                qb[:, 0:tn], waits=[(act, act.n)])
                            qev_free[par] = dqs[par].n
                        else:
                            hk = (o_g - D) // 128
                            E("a", (lambda e, bank=bank, t0=t0, tn=tn, hk=hk, o_g=o_g:
                                    e.activation(s_K[:, hk * T + t0: hk * T + t0 + tn],
                                                 pp[bank][:, 0:tn], AF.Identity,
                                                 bias=s_bq[:, o_g // 128: o_g // 128 + 1])),
                              waits=[(pe, pe_mm)], inc=act)
                            psA_free[bank] = act.n
                        evi[0] += 1
                else:
                    hv0 = o_g - 2 * D
                    for tc in range(TC):
                        bank = evi[0] % 2
                        mm_w = [(dwq[p % 2], pan_dg[p]),
                                (act, psA_free[bank])]
                        for dc in range(DC):
                            E("t", (lambda e, dc=dc, bank=bank, tc=tc, oc=oc, buf=buf:
                                    e.matmul(pp[bank][:, 0:128],
                                             s_xbf[:, dc * T + tc * 128: dc * T + tc * 128 + 128],
                                             buf[:, dc * PA + oc * 128: dc * PA + oc * 128 + 128],
                                             start=(dc == 0), stop=(dc == DC - 1))),
                              waits=(mm_w + [(dx, dx_all)]) if dc == 0 else [],
                              inc=pe if dc == DC - 1 else None)
                        pe_mm = pe.n
                        par = evi[0] % 2
                        qb = qev_bufs[par]
                        E("a", (lambda e, bank=bank, qb=qb:
                                e.activation(qb[:, 0:128], pp[bank][:, 0:128], AF.Copy)),
                          waits=[(pe, pe_mm), (dqs[par], qev_free[par])], inc=act)
                        psA_free[bank] = act.n
                        DMA("s", dqs[par], v_scr[tc * 128:(tc + 1) * 128, hv0:hv0 + 128],
                            qb[:, 0:128], waits=[(act, act.n)])
                        qev_free[par] = dqs[par].n
                        evi[0] += 1
            pan_pe_done[p] = pe.n
            if p + 2 < NPAN:
                load_panel(p + 2)

        endA = dict(pe=pe.n, act=act.n, dve=dve.n,
                    dqs0=dqs[0].n, dqs1=dqs[1].n)

        probsT_v = s_probsT[:, :].rearrange("p (c q) -> p c q", q=QBS)
        vp_v = [b[:, :].rearrange("p (c o) -> p c o", o=256) for b in vp_bufs]
        wo_v = [b[:, :].rearrange("p (c o) -> p c o", o=256) for b in wo_bufs]
        vscr_v = v_scr[:, :].rearrange("(c p) h -> p c h", p=128)
        qscr_v = q_scr[:, :].rearrange("(c p) t -> p c t", p=128)
        woutT_v = woutT[:, :].rearrange("(c p) o -> p c o", p=128)
        xT_v = xT[:, :].rearrange("(c p) t -> p c t", p=128)

        first_B = [(pe, endA["pe"]), (act, endA["act"]), (dve, endA["dve"]),
                   (dqs[0], endA["dqs0"]), (dqs[1], endA["dqs1"])]

        spill_counts = {}
        q_bufs = (s_q0, s_q1)
        pending_ln = [None]
        vp_gate = {0: endA["pe"], 1: endA["pe"]}
        wo_gate = {0: endA["pe"], 1: endA["pe"]}
        for qblk in range(NQB):
            tok0 = qblk * QBS
            dve_probs_prev = 0
            dsn_q = 0
            dve_m_prev = 0
            act_exp_prev = 0
            pe_tr_done = [0]
            pe_scores_qt = {}
            pending_tr = []

            def emit_tr(qt, dve_nrm):
                for tc in range(TC):
                    tb_ = tc % 2
                    E("t", (lambda e, tc=tc, tb_=tb_:
                            e.transpose(p_tr[tb_][:, :], s_probs[:, tc * 128:(tc + 1) * 128],
                                        s_id[:, :])),
                      waits=([(dve, dve_nrm), (act, bank_free.get(("tr", tb_), 0))]
                             if tc == 0 else [(act, bank_free.get(("tr", tb_), 0))]),
                      inc=pe)
                    E("a", (lambda e, tc=tc, qt=qt, tb_=tb_:
                            e.activation(probsT_v[:, tc, qt * 128:qt * 128 + 128],
                                         p_tr[tb_][:, :], AF.Copy)),
                      waits=[(pe, pe.n)], inc=act)
                    bank_free[("tr", tb_)] = act.n
                pe_tr_done[0] = pe.n

            NQT_B = QBS // 128
            dqr_at = {}

            def load_q(qt):
                par = qt % 2
                if qt == 0:
                    w = first_B + [(pe, bank_free.get("sq", 0))]
                elif qt == 1:
                    w = [(pe, bank_free.get("sq", 0))]
                else:
                    w = [(pe, pe_scores_qt[qt - 2])]
                DMA("s", dqr[par],
                    q_bufs[par][:, :].rearrange("p (c t) -> p c t", t=QW),
                    qscr_v[:, :, tok0 + qt * QW: tok0 + (qt + 1) * QW], waits=w)
                dqr_at[qt] = dqr[par].n

            load_q(0)
            if NQT_B > 1:
                load_q(1)
            for qt in range(NQT_B):
                qpar = qt % 2
                first_B = []
                dsn_q = dqr_at[qt]
                for ki, (k0, kn) in enumerate(KTB):
                    skb = ki % 2
                    E_w = [(act, max(bank_free.get(("scb", skb), 0),
                                     bank_free.get(("ctx", skb), 0)))]
                    for hc in range(DC):
                        E("t", (lambda e, skb=skb, hc=hc, k0=k0, kn=kn, qpar=qpar:
                                e.matmul(pp[skb][:, 0:kn],
                                         q_bufs[qpar][:, hc * QW: hc * QW + 128],
                                         s_K[:, hc * T + k0: hc * T + k0 + kn],
                                         start=(hc == 0), stop=(hc == DC - 1))),
                          waits=(([(dqr[qpar], dsn_q)] + E_w) if hc == 0 else []),
                          inc=pe if hc == DC - 1 else None)
                    E("a", (lambda e, skb=skb, k0=k0, kn=kn:
                            e.activation(s_scores[:, k0:k0 + kn], pp[skb][:, 0:kn],
                                         AF.Copy)),
                      waits=[(pe, pe.n), (dve, dve_m_prev), (act, act_exp_prev)],
                      inc=act)
                    bank_free[("scb", skb)] = act.n
                pe_scores_qt[qt] = pe.n
                if qt + 2 < NQT_B:
                    load_q(qt + 2)
                if pending_tr:
                    emit_tr(*pending_tr.pop())
                if pending_ln[0] is not None:
                    if next(pending_ln[0], "DONE") == "DONE":
                        pending_ln[0] = None
                act_sc = act.n
                E("v", (lambda e:
                        e.reduce_max(s_negm[:, 0:1], s_scores[:, :], AX.X, negate=True)),
                  waits=[(act, act_sc)], inc=dve)
                E("v", lambda e: e.tensor_scalar_mul(s_negm[:, 1:2], s_negm[:, 0:1], RSQH),
                  waits=[(dve, dve.n)], inc=dve)
                dve_m = dve.n
                dve_m_prev = dve_m
                E("a", (lambda e:
                        e.activation(s_probs[:, :], s_scores[:, :],
                                     AF.Exp, bias=s_negm[:, 1:2], scale=RSQH)),
                  waits=[(dve, dve_m), (pe, pe_tr_done[0])], inc=act)
                act_exp = act.n
                act_exp_prev = act_exp
                E("v", lambda e: e.reduce_sum(s_negm[:, 2:3], s_probs[:, :], AX.X),
                  waits=[(act, act_exp)] + ([(dqr, dqr.n)] if debug_outputs else []),
                  inc=dve)
                E("v", lambda e: e.reciprocal(s_negm[:, 3:4], s_negm[:, 2:3]),
                  waits=[(dve, dve.n)], inc=dve)
                E("v", lambda e: e.tensor_scalar_mul(s_probs[:, :], s_probs[:, :],
                                                     s_negm[:, 3:4]),
                  waits=[(dve, dve.n)], inc=dve)
                dve_nrm = dve.n
                if debug_outputs:
                    DMA("s", dqr, pb_scr[tok0 + qt * 128: tok0 + qt * 128 + 128, :],
                        s_probs[:, :], waits=[(dve, dve_nrm)])
                pending_tr.append((qt, dve_nrm))
            if pending_tr:
                emit_tr(*pending_tr.pop())
            act_probsT = act.n

            def load_vp(hcp):
                par = hcp % 2
                DMA("s", dvp[par], vp_v[par][:, :, :],
                    vscr_v[:, :, hcp * 256: hcp * 256 + 256],
                    waits=[(pe, vp_gate[par])])
                dsn_vps[hcp] = dvp[par].n

            dsn_vps = {}
            load_vp(0)
            if DC // 2 > 1:
                load_vp(1)

            if pending_ln[0] is not None:
                while next(pending_ln[0], "DONE") != "DONE":
                    pass
                pending_ln[0] = None
                bank_free["xlnfree"] = dve.n
            if qblk == NQB - 1 and NQB > 1:
                w0 = [(pe, pe.n), (dpre, pre_all),
                      (dso[0], dso[0].n), (dso[1], dso[1].n)]
                for dc in range(DC):
                    DMA("g", dacc, s_acc[:, dc * TO: dc * TO + TBS[0][1]],
                        x1_scr[dc * 128:(dc + 1) * 128, 0:TBS[0][1]], waits=w0)
                    w0 = []
                dacc_h0 = dacc.n

            for hcp in range(DC // 2):
                vpar = hcp % 2
                dsn_vp = dsn_vps[hcp]
                for hh in range(2):
                    hc = hcp * 2 + hh
                    bank = hc % 2
                    for tc in range(TC):
                        E("t", (lambda e, tc=tc, hh=hh, bank=bank, vpar=vpar:
                                e.matmul(pp[bank][:, 0:QBS],
                                         vp_v[vpar][:, tc, hh * 128: hh * 128 + 128],
                                         probsT_v[:, tc, :],
                                         start=(tc == 0), stop=(tc == TC - 1))),
                          waits=([(dvp[vpar], dsn_vp),
                                  (act, max(act_probsT, bank_free.get(("ctx", bank), 0),
                                            bank_free.get(("scb", bank), 0)))]
                                 if tc == 0 else []),
                          inc=pe if tc == TC - 1 else None)
                    E("a", (lambda e, hc=hc, bank=bank:
                            e.activation(s_ctx[:, hc * QBS:(hc + 1) * QBS],
                                         pp[bank][:, 0:QBS], AF.Identity,
                                         bias=s_bq[:, 2 * DC + hc: 2 * DC + hc + 1])),
                      waits=[(pe, pe.n)], inc=act)
                    bank_free[("ctx", bank)] = act.n
                    if debug_outputs:
                        DMA("s", dvp[vpar], ctx_scr[hc * 128:(hc + 1) * 128, tok0:tok0 + QBS],
                            s_ctx[:, hc * QBS:(hc + 1) * QBS], waits=[(act, act.n)])
                vp_gate[vpar] = pe.n
                if hcp + 2 < DC // 2:
                    load_vp(hcp + 2)
            act_ctx = act.n

            dg_wos = {}

            def load_wo(opan):
                par = opan % 2
                DMA("g", dwo[par], wo_v[par][:, :, :],
                    woutT_v[:, :, opan * 256: opan * 256 + 256],
                    waits=[(pe, wo_gate[par])])
                dg_wos[opan] = dwo[par].n

            load_wo(0)
            if DC // 2 > 1:
                load_wo(1)
            for opan in range(DC // 2):
                wpar = opan % 2
                dg_wo = dg_wos[opan]
                for oh in range(2):
                    oc = opan * 2 + oh
                    par = oc % 2
                    xow = xow_bufs[par]
                    DMA("s", dxw[par], xow[:, 0:QBS], xT_v[:, oc, tok0:tok0 + QBS],
                        waits=[(dve, bank_free.get(("xow", par), 0))])
                    dsn_x = dxw[par].n
                    bank = 2 + (oc % 2)
                    for hc in range(DC):
                        E("t", (lambda e, hc=hc, oh=oh, bank=bank, wpar=wpar:
                                e.matmul(pp[bank][:, 0:QBS],
                                         wo_v[wpar][:, hc, oh * 128: oh * 128 + 128],
                                         s_ctx[:, hc * QBS:(hc + 1) * QBS],
                                         start=(hc == 0), stop=(hc == DC - 1))),
                          waits=([(dwo[wpar], dg_wo),
                                  (act, max(act_ctx, bank_free.get(("wo", bank), 0))),
                                  (dve, bank_free.get(("lnbc", 0), 0))]
                                 if hc == 0 else []),
                          inc=pe if hc == DC - 1 else None)
                    E("a", (lambda e, oc=oc, bank=bank:
                            e.activation(s_xln[:, oc * QBS:(oc + 1) * QBS],
                                         pp[bank][:, 0:QBS], AF.Identity,
                                         bias=s_bo[:, oc:oc + 1])),
                      waits=[(pe, pe.n), (dve, bank_free.get("xlnfree", 0)),
                             (po, bank_free.get("xlnfree_po", 0))], inc=act)
                    bank_free[("wo", bank)] = act.n
                    E("v", (lambda e, oc=oc, xow=xow:
                            e.tensor_add(s_xln[:, oc * QBS:(oc + 1) * QBS],
                                         s_xln[:, oc * QBS:(oc + 1) * QBS],
                                         xow[:, 0:QBS])),
                      waits=[(act, act.n), (dxw[par], dsn_x)], inc=dve)
                    bank_free[("xow", par)] = dve.n
                    if debug_outputs:
                        DMA("s", dxw[par], h1_scr[oc * 128:(oc + 1) * 128, tok0:tok0 + QBS],
                            s_xln[:, oc * QBS:(oc + 1) * QBS], waits=[(dve, dve.n)])
                wo_gate[wpar] = pe.n
                if opan + 2 < DC // 2:
                    load_wo(opan + 2)
            dve_h1 = dve.n
            if qblk == NQB - 1:
                pe_wo_end = pe.n
                dve_h1_end = dve_h1

            if qblk == NQB - 1:
                assert pending_ln[0] is None
                ln1_apply_dve = run_ln(s_xln, 0, QBS, dve_h1, None, tok0, QBS,
                                       acc_dst=s_acc, acc_stride=TO)
            else:
                pending_ln[0] = layernorm_gen(s_xln, 0, QBS, dve_h1, x1_scr,
                                              tok0, QBS)
                next(pending_ln[0], None)
            bank_free["xlnfree"] = dve.n
            bank_free["xlnfree_po"] = po.n
            bank_free["sq"] = pe.n

        endB = dict(pe=pe.n, act=act.n, dve=dve.n, po=po.n,
                    dso0=dso[0].n, dso1=dso[1].n)

        dsn_acc = dacc.n
        first_x = [(pe, pe_wo_end), (act, endB["act"])]
        act_xlnb_dc = {}
        last_t0 = TBS[-1][0] if NQB > 1 else None
        for (t0r, tnr) in TBS:
            for dc in range(DC):
                w = list(first_x)
                first_x = []
                if t0r == last_t0 or NQB == 1:
                    w.append((dve, ln1_apply_dve[dc]))
                else:
                    w.append((dacc, dsn_acc))
                E("a", (lambda e, dc=dc, t0r=t0r, tnr=tnr:
                        e.activation(s_xlnb[:, dc * TO + t0r: dc * TO + t0r + tnr],
                                     s_acc[:, dc * TO + t0r: dc * TO + t0r + tnr], AF.Copy)),
                  waits=w, inc=act)
                act_xlnb_dc[(t0r, dc)] = act.n
        act_xlnb = act.n

        aup_v = s_aup[:, :].rearrange("p (c r) -> p c r", r=R)
        adn_v = s_adn[:, :].rearrange("p (c r) -> p c r", r=R)

        t1_emitted = [False]
        dve_t1_cell = [0]

        def emit_t1():
            for bi, (t0, tn) in enumerate(TBS):
                for dc in range(DC):
                    E("t", (lambda e, dc=dc, bi=bi, t0=t0, tn=tn:
                            e.matmul(pst[bi][0:R, 0:tn], aup_v[:, dc, :],
                                     s_xlnb[:, dc * TO + t0: dc * TO + t0 + tn],
                                     start=(dc == 0), stop=(dc == DC - 1),
                                     skip_group_check=True)),
                      waits=([(act, act_xlnb_dc[(t0, dc)])] +
                             ([(dve, bank_free.get("lnrows", 0))] if dc == 0 else [])),
                      inc=pe if dc == DC - 1 else None)
                E("v", (lambda e, bi=bi, t0=t0, tn=tn:
                        e.tensor_copy(s_t1[0:R, t0:t0 + tn], pst[bi][0:R, 0:tn])),
                  waits=[(pe, pe.n)], inc=dve)
            dve_t1_cell[0] = dve.n
            t1_emitted[0] = True

        uq_v = [b[:, :].rearrange("p (c o) -> p c o", o=128) for b in uq_bufs]
        dq_v = [b[:, :].rearrange("p (c o) -> p c o", o=128) for b in dq_bufs]
        wupq_v = wupqT[:, :].rearrange("(c p) h -> p c h", p=128)
        sexpu_v = sexpu[:, :].rearrange("(c p) h -> p c h", p=128)
        dnq_v = wdnqT[:, :].rearrange("(c p) d -> p c d", p=128)
        sexpd_v = sexpd[:, :].rearrange("(c p) d -> p c d", p=128)

        up_dg, up_dve, up_pe_done = {}, {}, {}
        dn_dg, dn_dve, dn_pe_done = {}, {}, {}

        NBUF = 3

        def load_up(u):
            w = ([(pe, up_pe_done[u - NBUF])] if u - NBUF in up_pe_done else
                 [(pe, pe_wo_end), (dve, dve_h1_end)])
            DMA("g", duq[u % NBUF], uq_v[u % NBUF][:, :, :],
                wupq_v[:, :, u * 128:(u + 1) * 128], waits=w)
            DMA("g", dus[u % NBUF],
                us_bufs[u % NBUF][:, :].rearrange("p (c o) -> p c o", o=128),
                sexpu_v[:, :, u * 128:(u + 1) * 128])
            up_dg[u] = (duq[u % NBUF].n, dus[u % NBUF].n)

        def deq_up(u, eng="v"):
            esem = dve if eng == "v" else po
            if eng == "v":
                E(eng, (lambda e, u=u:
                        e.scalar_tensor_tensor(uq_bufs[u % NBUF][:, :], uq_bufs[u % NBUF][:, :],
                                               7.5, us_bufs[u % NBUF][:, :],
                                               ALU.subtract, ALU.mult)),
                  waits=[(duq[u % NBUF], up_dg[u][0]), (dus[u % NBUF], up_dg[u][1])],
                  inc=esem)
            else:
                E(eng, (lambda e, u=u:
                        e.tensor_scalar_add(uq_bufs[u % NBUF][:, :],
                                            uq_bufs[u % NBUF][:, :], -7.5)),
                  waits=[(duq[u % NBUF], up_dg[u][0])], inc=esem)
                E(eng, (lambda e, u=u:
                        e.tensor_mul(uq_bufs[u % NBUF][:, :], uq_bufs[u % NBUF][:, :],
                                     us_bufs[u % NBUF][:, :])),
                  waits=[(dus[u % NBUF], up_dg[u][1]), (esem, esem.n)], inc=esem)
            up_dve[u] = (esem, esem.n)

        def load_dn(gi, dc):
            key = gi * DC + dc
            w = ([(pe, dn_pe_done[key - NBUF])] if key - NBUF in dn_pe_done else
                 [(pe, endB["pe"]), (act, endB["act"]), (dve, endB["dve"]),
                  (po, endB["po"])])
            DMA("g", ddn[key % NBUF], dq_v[key % NBUF][:, :, :],
                dnq_v[:, gi * GS:(gi + 1) * GS, dc * 128:(dc + 1) * 128], waits=w)
            DMA("g", dds[key % NBUF],
                ds_bufs[key % NBUF][:, :].rearrange("p (c o) -> p c o", o=128),
                sexpd_v[:, gi * GS:(gi + 1) * GS, dc * 128:(dc + 1) * 128])
            dn_dg[key] = (ddn[key % NBUF].n, dds[key % NBUF].n)

        def deq_dn(key):
            E("v", (lambda e, key=key:
                    e.scalar_tensor_tensor(dq_bufs[key % NBUF][:, :], dq_bufs[key % NBUF][:, :],
                                           7.5, ds_bufs[key % NBUF][:, :],
                                           ALU.subtract, ALU.mult)),
              waits=[(ddn[key % NBUF], dn_dg[key][0]), (dds[key % NBUF], dn_dg[key][1])],
              inc=dve)
            dn_dve[key] = dve.n

        for u0 in range(min(NBUF, GS * NG)):
            load_up(u0)

        E("v", lambda e: e.memset(s_bupl[:, :], 0.0),
          waits=[(pe, endB["pe"]), (act, endB["act"]), (dve, endB["dve"]),
                 (po, endB["po"])], inc=dve)
        E("v", lambda e: e.memset(s_bdnl[:, :], 0.0), waits=[(dve, dve.n)], inc=dve)
        dve_lbz = dve.n
        DMA("g", dlb, s_bupl[0:R, :], bupl[:, :],
            waits=[(dve, dve_lbz), (dpre, pre_all)])
        DMA("g", dlb, s_bdnl[0:R, :], bdnl[:, :])
        dlb_all = dlb.n


        for gi in range(NG):
            for i in range(GS):
                u = gi * GS + i
                if u not in up_dve:
                    deq_up(u)
                if u + 1 < GS * NG and (u + 1) not in up_dve:
                    deq_up(u + 1)
                for bi, (t0, tn) in enumerate(TBS):
                    bank = (len(TBS) * u + bi) % 4
                    for dc in range(DC):
                        w_mm = []
                        if dc == 0:
                            w_mm = [up_dve[u],
                                    (dve, bank_free.get(("ppV", bank), 0)),
                                    (act, max(bank_free.get(("ppA", bank), 0),
                                              endB["act"] if u < 2 else 0))]
                        if not t1_emitted[0]:
                            w_mm = w_mm + [(act, act_xlnb_dc[(t0, dc)])]
                        E("t", (lambda e, dc=dc, u=u, bank=bank, t0=t0, tn=tn:
                                e.matmul(pp[bank][:, 0:tn], uq_v[u % NBUF][:, dc, :],
                                         s_xlnb[:, dc * TO + t0: dc * TO + t0 + tn],
                                         start=(dc == 0), stop=False)),
                          waits=w_mm)
                    if not t1_emitted[0]:
                        emit_t1()
                    E("t", (lambda e, u=u, bank=bank, t0=t0, tn=tn:
                            e.matmul(pp[bank][:, 0:tn],
                                     s_bupl[:, u * 128:(u + 1) * 128],
                                     s_t1[:, t0:t0 + tn], start=False, stop=True)),
                      waits=([(dlb, dlb_all), (dve, dve_t1_cell[0])]
                             if u == 0 else []), inc=pe)
                    E("a", (lambda e, i=i, u=u, bank=bank, t0=t0, tn=tn:
                            e.activation(s_x2[:, i * TO + t0: i * TO + t0 + tn],
                                         pp[bank][:, 0:tn], AF.Relu,
                                         bias=s_bu[:, u:u + 1])),
                      waits=[(pe, pe.n)], inc=act)
                    bank_free[("ppA", bank)] = act.n
                up_pe_done[u] = pe.n
                if u + NBUF < GS * NG:
                    load_up(u + NBUF)
            act_x2 = act.n

            for bi, (t0, tn) in enumerate(TBS):
                for i in range(GS):
                    hcg = gi * GS + i
                    E("t", (lambda e, i=i, hcg=hcg, bi=bi, t0=t0, tn=tn, gi=gi:
                            e.matmul(pst[bi][0:R, 0:tn], adn_v[:, hcg, :],
                                     s_x2[:, i * TO + t0: i * TO + t0 + tn],
                                     start=(gi == 0 and i == 0),
                                     stop=(gi == NG - 1 and i == GS - 1),
                                     skip_group_check=True)),
                      waits=([(act, act_x2), (dve, dve_t1_cell[0])] if i == 0 else []),
                      inc=pe if (gi == NG - 1 and i == GS - 1 and bi == len(TBS) - 1) else None)

            if gi == 0:
                for d0 in range(min(NBUF, DC)):
                    load_dn(0, d0)
            for dc in range(DC):
                key = gi * DC + dc
                if dc == 0 and gi == 0:
                    deq_dn(key)
                for k2 in (key + 1, key + 2):
                    if k2 < NG * DC and k2 <= key + NBUF - 1 and                        k2 > max(dn_dve, default=-1):
                        deq_dn(k2)
                for bi, (t0, tn) in enumerate(TBS):
                    bank = (len(TBS) * dc + bi) % 4
                    for i in range(GS):
                        E("t", (lambda e, i=i, key=key, bank=bank, t0=t0, tn=tn:
                                e.matmul(pp[bank][:, 0:tn], dq_v[key % NBUF][:, i, :],
                                         s_x2[:, i * TO + t0: i * TO + t0 + tn],
                                         start=(i == 0), stop=(i == GS - 1))),
                          waits=([(act, max(act_x2, bank_free.get(("ppA", bank), 0))),
                                  (dve, max(dn_dve[key],
                                            bank_free.get(("ppV", bank), 0)))]
                                 if i == 0 else []),
                          inc=pe if i == GS - 1 else None)
                    E("v", (lambda e, dc=dc, bank=bank, t0=t0, tn=tn:
                            e.tensor_add(s_acc[:, dc * TO + t0: dc * TO + t0 + tn],
                                         pp[bank][:, 0:tn],
                                         s_acc[:, dc * TO + t0: dc * TO + t0 + tn])),
                      waits=[(pe, pe.n), (dacc, dsn_acc)], inc=dve)
                    bank_free[("ppV", bank)] = dve.n
                dn_pe_done[key] = pe.n
                if dc + NBUF < DC:
                    load_dn(gi, dc + NBUF)
                elif gi + 1 < NG and dc + NBUF - DC < min(NBUF, DC):
                    load_dn(gi + 1, dc + NBUF - DC)

        acc_done = {}
        for bi, (t0, tn) in enumerate(TBS):
            E("v", (lambda e, bi=bi, t0=t0, tn=tn:
                    e.tensor_copy(s_t2[0:R, t0:t0 + tn], pst[bi][0:R, 0:tn])),
              waits=[(pe, pe.n)], inc=dve)
        dve_t2 = dve.n
        for dc in range(DC):
            for bi, (t0, tn) in enumerate(TBS):
                bank = (len(TBS) * dc + bi) % 4
                E("t", (lambda e, dc=dc, bank=bank, t0=t0, tn=tn:
                        e.matmul(pp[bank][:, 0:tn], s_bdnl[:, dc * 128:(dc + 1) * 128],
                                 s_t2[:, t0:t0 + tn], start=True, stop=True)),
                  waits=[(dve, max(dve_t2, bank_free.get(("ppV", bank), 0),
                                   bank_free.get(("lb", bank), 0))),
                         (act, bank_free.get(("ppA", bank), 0)),
                         (dlb, dlb_all)], inc=pe)
                E("v", (lambda e, dc=dc, bank=bank, t0=t0, tn=tn:
                        e.scalar_tensor_tensor(
                            s_acc[:, dc * TO + t0: dc * TO + t0 + tn],
                            pp[bank][:, 0:tn], s_bd[:, dc:dc + 1],
                            s_acc[:, dc * TO + t0: dc * TO + t0 + tn],
                            ALU.add, ALU.add)),
                  waits=[(pe, pe.n)], inc=dve)
                bank_free[("lb", bank)] = dve.n
                bank_free[("ppV", bank)] = dve.n
            acc_done[dc] = (dve, dve.n)
        dve_acc = dve.n

        for (t0, tn) in TBS:
            run_ln(s_acc, t0, tn, dve_acc, y, t0, TO, ready_map=acc_done)

        E("s", None, waits=[(dso[0], dso[0].n), (dso[1], dso[1].n)])

        def replay(key):
            def f(eng):
                for fn, waits, incs in ops[key]:
                    for h, v in waits:
                        eng.wait_ge(h, v)
                    if fn is not None:
                        ins = fn(eng)
                        for h, d in incs:
                            ins.then_inc(h, d)
            return f

        with nc.Block() as blk:
            blk.gpsimd(replay("g"))
            blk.sync(replay("s"))
            blk.tensor(replay("t"))
            blk.vector(replay("v"))
            blk.scalar(replay("a"))

    return nc


def make_core_inputs(cfg, x_b, w_qkv, b_qkv, w_out, b_out, w_up_q, w_up_scale,
                     b_up, a_up, b_upl, w_dn_q, w_dn_scale, b_dn, a_dn, b_dnl,
                     half, shared):
    D, T, TO, FF, R = cfg["D"], cfg["T"], cfg["TO"], cfg["FF"], cfg["R"]
    f32 = np.float32
    xTb = np.ascontiguousarray(x_b.T.astype(f32, copy=False))
    if half == 1:
        xTb = np.ascontiguousarray(
            np.concatenate([xTb[:, TO:], xTb[:, :TO]], axis=1))
    ins = {"xT": xTb}
    ins.update(shared)
    return ins


def make_shared_inputs(cfg, w_qkv, b_qkv, w_out, b_out, w_up_q, w_up_scale,
                       b_up, a_up, b_upl, w_dn_q, w_dn_scale, b_dn, a_dn,
                       b_dnl):
    D, FF = cfg["D"], cfg["FF"]
    BLK = 64
    f32 = np.float32

    def pp_(v, n):
        return np.ascontiguousarray(np.asarray(v, dtype=f32).reshape(n, 128).T)

    return {
        "wqkvT": np.ascontiguousarray(np.asarray(w_qkv, dtype=f32).T),
        "woutT": np.ascontiguousarray(np.asarray(w_out, dtype=f32).T),
        "bqkv_pp": pp_(b_qkv, 3 * D // 128),
        "bout_pp": pp_(b_out, D // 128),
        "wupqT": np.ascontiguousarray(np.asarray(w_up_q, dtype=np.int32).T),
        "sexpu": np.ascontiguousarray(
            np.repeat(np.asarray(w_up_scale, dtype=f32).T, BLK, axis=0)),
        "bup_pp": pp_(b_up, FF // 128),
        "aup": np.ascontiguousarray(np.asarray(a_up, dtype=f32)),
        "bupl": np.ascontiguousarray(np.asarray(b_upl, dtype=f32)),
        "wdnqT": np.ascontiguousarray(np.asarray(w_dn_q, dtype=np.int32).T),
        "sexpd": np.ascontiguousarray(
            np.repeat(np.asarray(w_dn_scale, dtype=f32).T, BLK, axis=0)),
        "bdn_pp": pp_(b_dn, D // 128),
        "adn": np.ascontiguousarray(np.asarray(a_dn, dtype=f32)),
        "bdnl": np.ascontiguousarray(np.asarray(b_dnl, dtype=f32)),
        "ident": np.eye(128, dtype=f32),
    }


def kernel(L0_attn_hidden_states, L0_attn_w_qkv, L0_attn_b_qkv, L0_attn_w_out,
           L0_attn_b_out, L0_ffn_w_up_q, L0_ffn_w_up_scale, L0_ffn_b_up,
           L0_ffn_w_up_lora_a, L0_ffn_w_up_lora_b, L0_ffn_w_down_q,
           L0_ffn_w_down_scale, L0_ffn_b_down, L0_ffn_w_down_lora_a,
           L0_ffn_w_down_lora_b):
    cfg = CFG_FULL
    D, T, TO = cfg["D"], cfg["T"], cfg["TO"]
    x = np.asarray(L0_attn_hidden_states, dtype=np.float32)
    B = x.shape[0]

    nc = build_program(cfg)
    shared = make_shared_inputs(
        cfg, L0_attn_w_qkv, L0_attn_b_qkv, L0_attn_w_out, L0_attn_b_out,
        L0_ffn_w_up_q, L0_ffn_w_up_scale, L0_ffn_b_up, L0_ffn_w_up_lora_a,
        L0_ffn_w_up_lora_b, L0_ffn_w_down_q, L0_ffn_w_down_scale,
        L0_ffn_b_down, L0_ffn_w_down_lora_a, L0_ffn_w_down_lora_b)

    in_maps = []
    for core in range(8):
        b, half = core // 2, core % 2
        in_maps.append(make_core_inputs(
            cfg, x[b], None, None, None, None, None, None, None, None, None,
            None, None, None, None, None, half, shared))

    res = run_bass_kernel_spmd(nc, in_maps, core_ids=list(range(8)))

    out = np.empty((B, T, D), dtype=np.float32)
    for core in range(8):
        b, half = core // 2, core % 2
        out[b, half * TO:(half + 1) * TO, :] = res.results[core]["y"].T
    return out


# version 22
# speedup vs baseline: 1.1242x; 1.0201x over previous
import contextlib
import numpy as np
import concourse.bass as bass
import concourse.mybir as mybir
from concourse.bass_utils import run_bass_kernel_spmd

F32 = mybir.dt.float32
BF16 = mybir.dt.bfloat16
I32 = mybir.dt.int32
I8 = mybir.dt.int8
AF = mybir.ActivationFunctionType
ALU = mybir.AluOpType
AX = mybir.AxisListType

EPS = 1e-5
CFG_FULL = dict(D=2048, T=2048, TO=1024, FF=8192, R=16)


class Sem:
    def __init__(self, handle):
        self.h = handle
        self.n = 0


def build_program(cfg, debug_outputs=False):
    nc = bass.Bass("TRN2", target_bir_lowering=False)
    scratch_kind = "ExternalOutput" if debug_outputs else "Internal"
    D, T, TO, FF, R = cfg["D"], cfg["T"], cfg["TO"], cfg["FF"], cfg["R"]
    DC, FC, TC = D // 128, FF // 128, T // 128
    RSQH = float(1.0 / np.sqrt(D))

    def blocks(n):
        out, i = [], 0
        while i < n:
            out.append((i, min(512, n - i)))
            i += 512
        return out

    TBS = blocks(TO)
    KTB = blocks(T)
    QBS = min(512, TO)
    NQB = TO // QBS
    GS = min(8, FC)
    NG = FC // GS
    PA = min(256, D)
    NPAN = 3 * D // PA

    dram = nc.dram_tensor
    xT = dram("xT", [D, T], F32, kind="ExternalInput")
    wqkvT = dram("wqkvT", [D, 3 * D], F32, kind="ExternalInput")
    woutT = dram("woutT", [D, D], F32, kind="ExternalInput")
    bqkv_pp = dram("bqkv_pp", [128, 3 * DC], F32, kind="ExternalInput")
    bout_pp = dram("bout_pp", [128, DC], F32, kind="ExternalInput")
    wupqT = dram("wupqT", [D, FF], I8, kind="ExternalInput")
    sexpu = dram("sexpu", [D, FF], F32, kind="ExternalInput")
    bup_pp = dram("bup_pp", [128, FC], F32, kind="ExternalInput")
    aup = dram("aup", [D, R], F32, kind="ExternalInput")
    bupl = dram("bupl", [R, FF], F32, kind="ExternalInput")
    wdnqT = dram("wdnqT", [FF, D], I8, kind="ExternalInput")
    sexpd = dram("sexpd", [FF, D], F32, kind="ExternalInput")
    bdn_pp = dram("bdn_pp", [128, DC], F32, kind="ExternalInput")
    adn = dram("adn", [FF, R], F32, kind="ExternalInput")
    bdnl = dram("bdnl", [R, D], F32, kind="ExternalInput")
    ident = dram("ident", [128, 128], F32, kind="ExternalInput")
    y = dram("y", [D, TO], F32, kind="ExternalOutput")
    q_scr = dram("q_scr", [D, TO], BF16, kind=scratch_kind)
    if debug_outputs:
        ctx_scr = dram("ctx_scr", [D, TO], BF16, kind="ExternalOutput")
        h1_scr = dram("h1_scr", [D, TO], F32, kind="ExternalOutput")
        pb_scr = dram("pb_scr", [TO, T], BF16, kind="ExternalOutput")
    v_scr = dram("v_scr", [T, D], BF16, kind=scratch_kind)
    x1_scr = dram("x1_scr", [D, TO], F32, kind=scratch_kind)

    LIMIT = 211200
    ARENA = nc.bump_sbuf(LIMIT)[0]
    bump = [0]

    def sb(name, cols, dtype, at=None):
        esz = 2 if dtype == BF16 else 4
        if at is None:
            at = bump[0]
            bump[0] = at + ((cols * esz + 31) // 32) * 32
            assert bump[0] <= LIMIT, f"{name} overflows sbuf ({bump[0]})"
            end = bump[0]
        else:
            end = at + ((cols * esz + 31) // 32) * 32
            assert end <= LIMIT, f"{name} overflows sbuf ({end})"
        t = nc.alloc_sbuf_tensor_at(name, [128, cols], dtype, offset=ARENA + at)
        return t, end

    def sbp(name, cols, dtype):
        t, _ = sb(name, cols, dtype)
        return t

    s_bq = sbp("s_bq", 3 * DC, F32)
    s_bo = sbp("s_bo", DC, F32)
    s_bu = sbp("s_bu", FC, F32)
    s_bd = sbp("s_bd", DC, F32)
    s_aup = sbp("s_aup", DC * R, BF16)
    s_adn = sbp("s_adn", FC * R, BF16)
    s_t1 = sbp("s_t1", TO, BF16)
    s_t2 = sbp("s_t2", TO, BF16)
    s_ones1 = sbp("s_ones1", 128, F32)
    s_onesb = sbp("s_onesb", 16, BF16)
    s_id = sbp("s_id", 128, BF16)
    s_rows = sbp("s_rows", 512 * 3, F32)
    s_h1b = sbp("s_h1b", 512 * 2, BF16)
    s_h1sq = sbp("s_h1sq", 512 * 2, BF16)
    s_tmp = sbp("s_tmp", 512, F32)
    s_negm = sbp("s_negm", 8, F32)
    s_oev = sbp("s_oev", 512 * 2, F32)
    s_bcast = sbp("s_bcast", 512 * 2, F32)

    P0 = ((bump[0] + 63) // 64) * 64

    s_K, pend = sb("s_K", DC * T, BF16, at=P0)
    PK = ((pend + 63) // 64) * 64

    s_xbf, a1 = sb("s_xbf", DC * T, BF16, at=PK)
    s_wq0, a2 = sb("s_wq0", DC * PA, BF16, at=a1)
    s_wq1, a3 = sb("s_wq1", DC * PA, BF16, at=a2)
    s_qev0, a4 = sb("s_qev0", 512, BF16, at=a3)
    s_qev1, a5 = sb("s_qev1", 512, BF16, at=a4)

    QW = 128
    s_ctx, b1 = sb("s_ctx", DC * QBS, BF16, at=PK)
    s_q0, _ = sb("s_q0", DC * QW, BF16, at=PK)
    s_q1, _q1e = sb("s_q1", DC * QW, BF16, at=PK + ((DC * QW * 2 + 31) // 32) * 32)
    s_probs, b2 = sb("s_probs", T, BF16, at=b1)
    s_probsT, b3 = sb("s_probsT", TC * QBS, BF16, at=b2)
    s_vp0, b4 = sb("s_vp0", TC * 256, BF16, at=b3)
    s_vp1, b5 = sb("s_vp1", TC * 256, BF16, at=b4)
    s_wo0, b5b = sb("s_wo0", DC * 256, BF16, at=b5)
    s_wo1, b6 = sb("s_wo1", DC * 256, BF16, at=b5b)
    s_xow0, b7 = sb("s_xow0", QBS, F32, at=b6)
    s_xow1, b8 = sb("s_xow1", QBS, F32, at=b7)
    s_xln, b9 = sb("s_xln", DC * QBS, F32, at=b8)
    s_scores, b10 = sb("s_scores", T, F32, at=b9)

    s_acc, c1 = sb("s_acc", DC * TO, F32, at=P0)
    s_xlnb, c2 = sb("s_xlnb", DC * TO, BF16, at=c1)
    s_x2, c3 = sb("s_x2", GS * TO, BF16, at=c2)
    s_uq0, c4 = sb("s_uq0", DC * 128, BF16, at=c3)
    s_uq1, c5 = sb("s_uq1", DC * 128, BF16, at=c4)
    s_uq2, c5b = sb("s_uq2", DC * 128, BF16, at=c5)
    s_us0, c6 = sb("s_us0", DC * 128, BF16, at=c5b)
    s_us1, c7 = sb("s_us1", DC * 128, BF16, at=c6)
    s_us2, c7b = sb("s_us2", DC * 128, BF16, at=c7)
    s_dq0, c8 = sb("s_dq0", GS * 128, BF16, at=c7b)
    s_dq1, c9 = sb("s_dq1", GS * 128, BF16, at=c8)
    s_dq2, c9b = sb("s_dq2", GS * 128, BF16, at=c9)
    s_ds0, c10 = sb("s_ds0", GS * 128, BF16, at=c9b)
    s_ds1, c11 = sb("s_ds1", GS * 128, BF16, at=c10)
    s_ds2, c11b = sb("s_ds2", GS * 128, BF16, at=c11)
    s_bupl, c12 = sb("s_bupl", FF, BF16, at=c11b)
    s_bdnl, c13 = sb("s_bdnl", D, BF16, at=c12)

    ops = {k: [] for k in ("t", "v", "a", "g", "s")}

    def E(eng, fn, waits=(), inc=None, inc_by=1):
        incs = []
        if inc is not None:
            inc.n += inc_by
            incs.append((inc.h, inc_by))
        ops[eng].append((fn, [(s.h, v) for s, v in waits if v > 0], incs))

    def DMA(eng, sem, dst, src, waits=()):
        E(eng, lambda e: e.dma_start(dst, src), waits, inc=sem, inc_by=16)

    with contextlib.ExitStack() as stack:
        en = stack.enter_context
        pp = [en(nc.psum_tensor(f"pp{i}", [128, 512], F32)) for i in range(4)]
        p_tr = [en(nc.psum_tensor("p_tr0", [128, 128], BF16)),
                en(nc.psum_tensor("p_tr1", [128, 128], BF16))]
        pst = [en(nc.psum_tensor(f"pst{i}", [16, 512], F32)) for i in range(2)]
        def S(name):
            return Sem(en(nc.semaphore(name)))
        pe, act, dve, po = S("pe"), S("act"), S("dve"), S("po")
        dpre, dx = S("dpre"), S("dx")
        dwq = [S("dwq0"), S("dwq1")]
        dqs = [S("dqs0"), S("dqs1")]
        dqr = [S("dqr0"), S("dqr1")]
        dvp = [S("dvp0"), S("dvp1")]
        dwo = [S("dwo0"), S("dwo1")]
        dxw = [S("dxw0"), S("dxw1")]
        dso = [S("dso0"), S("dso1")]
        dacc = S("dacc")
        duq = [S("duq0"), S("duq1"), S("duq2")]
        dus = [S("dus0"), S("dus1"), S("dus2")]
        ddn = [S("ddn0"), S("ddn1"), S("ddn2")]
        dds = [S("dds0"), S("dds1"), S("dds2")]
        dlb = S("dlb")

        wq_bufs = (s_wq0, s_wq1)
        qev_bufs = (s_qev0, s_qev1)
        xow_bufs = (s_xow0, s_xow1)
        uq_bufs, us_bufs = (s_uq0, s_uq1, s_uq2), (s_us0, s_us1, s_us2)
        dq_bufs, ds_bufs = (s_dq0, s_dq1, s_dq2), (s_ds0, s_ds1, s_ds2)
        vp_bufs = (s_vp0, s_vp1)
        wo_bufs = (s_wo0, s_wo1)

        bank_free = {}

        def run_ln(*args, **kw):
            res = {}
            for _ in layernorm_gen(*args, result=res, **kw):
                pass
            return res

        def layernorm_gen(src, tb0, tn, ready_dve, out_dram, out_col0, src_stride,
                          acc_dst=None, acc_stride=None, result=None,
                          ready_map=None):
            MEAN, VARC, STDC = 0, 512, 1024
            invD = 1.0 / D
            for dc in range(DC):
                par = dc % 2
                if ready_map is not None:
                    w = [ready_map[dc], (pe, bank_free.get(("lnst", par), 0))]
                else:
                    w = ([(dve, ready_dve)] if dc == 0 else []) + \
                        [(pe, bank_free.get(("lnst", par), 0))]
                E("a", (lambda e, dc=dc, par=par:
                        e.activation(s_h1b[:, par * 512: par * 512 + tn],
                                     src[:, dc * src_stride + tb0: dc * src_stride + tb0 + tn],
                                     AF.Copy)), waits=w, inc=act)
                E("a", (lambda e, dc=dc, par=par:
                        e.activation(s_h1sq[:, par * 512: par * 512 + tn],
                                     src[:, dc * src_stride + tb0: dc * src_stride + tb0 + tn],
                                     AF.Square)), inc=act)
                act_h = act.n
                E("t", (lambda e, dc=dc, par=par:
                        e.matmul(pst[0][0:1, 0:tn], s_onesb[:, 0:1],
                                 s_h1b[:, par * 512: par * 512 + tn],
                                 start=(dc == 0), stop=(dc == DC - 1),
                                 skip_group_check=True)),
                  waits=([(act, act_h)] +
                         ([(dve, bank_free.get("lnrows", 0))] if dc == 0 else [])))
                E("t", (lambda e, dc=dc, par=par:
                        e.matmul(pst[1][0:1, 0:tn], s_onesb[:, 0:1],
                                 s_h1sq[:, par * 512: par * 512 + tn],
                                 start=(dc == 0), stop=(dc == DC - 1),
                                 skip_group_check=True)), inc=pe)
                bank_free[("lnst", par)] = pe.n
                if dc % 4 == 3:
                    yield
            pe_stats = pe.n

            E("v", (lambda e: e.tensor_scalar_mul(
                s_rows[0:1, MEAN:MEAN + tn], pst[0][0:1, 0:tn], invD)),
              waits=[(pe, pe_stats)], inc=dve)
            E("v", (lambda e: e.tensor_scalar_mul(
                s_rows[0:1, VARC:VARC + tn], pst[1][0:1, 0:tn], invD)), inc=dve)
            E("a", (lambda e: e.activation(
                s_rows[0:1, STDC:STDC + tn], s_rows[0:1, MEAN:MEAN + tn], AF.Square)),
              waits=[(dve, dve.n)], inc=act)
            E("v", (lambda e: e.tensor_sub(
                s_rows[0:1, VARC:VARC + tn], s_rows[0:1, VARC:VARC + tn],
                s_rows[0:1, STDC:STDC + tn])), waits=[(act, act.n)], inc=dve)
            E("a", (lambda e: e.activation(
                s_rows[0:1, STDC:STDC + tn], s_rows[0:1, VARC:VARC + tn],
                AF.Sqrt, bias=s_negm[0:1, 7:8])), waits=[(dve, dve.n)], inc=act)
            E("v", (lambda e: e.reciprocal(
                s_rows[0:1, VARC:VARC + tn], s_rows[0:1, STDC:STDC + tn])),
              waits=[(act, act.n)], inc=dve)
            E("v", (lambda e: e.tensor_mul(
                s_rows[0:1, STDC:STDC + tn], s_rows[0:1, MEAN:MEAN + tn],
                s_rows[0:1, VARC:VARC + tn])), waits=[(dve, dve.n)], inc=dve)
            dve_rows = dve.n
            bank_free["lnrows"] = dve_rows

            yield
            E("t", (lambda e: e.matmul(pp[2][:, 0:tn], s_ones1[0:1, :],
                                       s_rows[0:1, VARC:VARC + tn],
                                       start=True, stop=True, skip_group_check=True)),
              waits=[(dve, max(dve_rows, bank_free.get(("lnbc", 0), 0),
                               bank_free.get(("ppV", 2), 0))),
                     (act, max(bank_free.get(("wo", 2), 0),
                               bank_free.get(("ppA", 2), 0)))])
            E("t", (lambda e: e.matmul(pp[3][:, 0:tn], s_ones1[0:1, :],
                                       s_rows[0:1, STDC:STDC + tn],
                                       start=True, stop=True, skip_group_check=True)),
              waits=[(dve, max(bank_free.get(("lnbc", 1), 0),
                               bank_free.get(("ppV", 3), 0))),
                     (act, bank_free.get(("ppA", 3), 0))], inc=pe)
            pe_bc = pe.n
            E("v", (lambda e: e.tensor_copy(s_bcast[:, 0:tn], pp[2][:, 0:tn])),
              waits=[(pe, pe_bc), (po, bank_free.get("ln_po", 0))], inc=dve)
            E("v", (lambda e: e.tensor_copy(s_bcast[:, 512:512 + tn], pp[3][:, 0:tn])),
              waits=[(dve, dve.n)], inc=dve)
            dve_bc = dve.n
            bank_free[("lnbc", 0)] = dve_bc
            bank_free[("lnbc", 1)] = dve_bc

            apply_dve = {}
            for dc in range(DC):
                par = dc % 2
                eng = "v"
                esem = dve
                if acc_dst is None:
                    dst = s_oev[:, par * 512: par * 512 + tn]
                else:
                    dst = acc_dst[:, dc * acc_stride + out_col0:
                                  dc * acc_stride + out_col0 + tn]
                w_mul = [(dve, dve_bc), (esem, esem.n)]
                if acc_dst is None:
                    w_mul.append((dso[par], bank_free.get(("lnout", par), 0)))
                E(eng, (lambda e, dc=dc, dst=dst:
                        e.tensor_mul(dst,
                                     src[:, dc * src_stride + tb0: dc * src_stride + tb0 + tn],
                                     s_bcast[:, 0:tn])),
                  waits=w_mul, inc=esem)
                E(eng, (lambda e, dst=dst:
                        e.tensor_sub(dst, dst, s_bcast[:, 512:512 + tn])),
                  waits=[(esem, esem.n)], inc=esem)
                if acc_dst is None:
                    DMA("s", dso[par],
                        out_dram[dc * 128:(dc + 1) * 128, out_col0:out_col0 + tn],
                        s_oev[:, par * 512: par * 512 + tn], waits=[(esem, esem.n)])
                    bank_free[("lnout", par)] = dso[par].n
                apply_dve[dc] = dve.n
            bank_free["ln_po"] = po.n
            if result is not None:
                result.update(apply_dve)

        E("v", lambda e: e.memset(s_t1[:, :], 0.0), inc=dve)
        E("v", lambda e: e.memset(s_t2[:, :], 0.0), inc=dve)
        E("v", lambda e: e.memset(s_ones1[:, :], 0.0), inc=dve)
        E("v", lambda e: e.memset(s_rows[:, :], 0.0), inc=dve)
        E("v", lambda e: e.memset(s_ones1[0:1, :], 1.0), waits=[(dve, dve.n)], inc=dve)
        E("v", lambda e: e.memset(s_onesb[:, 0:1], 1.0), inc=dve)
        E("v", lambda e: e.memset(s_negm[:, 7:8], EPS), inc=dve)
        pre_dve = dve.n

        DMA("g", dpre, s_bq[:, :], bqkv_pp[:, :])
        DMA("g", dpre, s_bo[:, :], bout_pp[:, :])
        DMA("g", dpre, s_bu[:, :], bup_pp[:, :])
        DMA("g", dpre, s_bd[:, :], bdn_pp[:, :])
        DMA("g", dpre, s_id[:, :], ident[:, :])
        DMA("g", dpre, s_aup[:, :].rearrange("p (c r) -> p c r", r=R),
            aup[:, :].rearrange("(c p) r -> p c r", p=128))
        DMA("g", dpre, s_adn[:, :].rearrange("p (c r) -> p c r", r=R),
            adn[:, :].rearrange("(c p) r -> p c r", p=128))
        pre_all = dpre.n

        dx_at = {}
        for dc in range(DC):
            DMA("g", dx, s_xbf[:, dc * T:(dc + 1) * T], xT[dc * 128:(dc + 1) * 128, :])
            dx_at[dc] = dx.n
        dx_all = dx.n

        wqv = [b[:, :].rearrange("p (c o) -> p c o", o=PA) for b in wq_bufs]
        wqkv_v = wqkvT[:, :].rearrange("(c p) o -> p c o", p=128)

        pan_dg, pan_pe_done = {}, {}
        psA_free = {0: 0, 1: 0}
        qev_free = {0: 0, 1: 0}
        evi = [0]

        def load_panel(p):
            w = [(pe, pan_pe_done[p - 2])] if p - 2 in pan_pe_done else []
            DMA("g", dwq[p % 2], wqv[p % 2][:, :, :],
                wqkv_v[:, :, p * PA:(p + 1) * PA], waits=w)
            pan_dg[p] = dwq[p % 2].n

        load_panel(0)
        if NPAN > 1:
            load_panel(1)

        for p in range(NPAN):
            buf = wq_bufs[p % 2]
            for oc in range(PA // 128):
                o_g = p * PA + oc * 128
                if o_g < 2 * D:
                    is_q = o_g < D
                    for (t0, tn) in (TBS if is_q else KTB):
                        bank = evi[0] % 2
                        mm_w = [(dwq[p % 2], pan_dg[p]),
                                (dpre, pre_all), (act, psA_free[bank])]
                        for dc in range(DC):
                            E("t", (lambda e, dc=dc, bank=bank, t0=t0, tn=tn, oc=oc, buf=buf:
                                    e.matmul(pp[bank][:, 0:tn],
                                             buf[:, dc * PA + oc * 128: dc * PA + oc * 128 + 128],
                                             s_xbf[:, dc * T + t0: dc * T + t0 + tn],
                                             start=(dc == 0), stop=(dc == DC - 1))),
                              waits=(mm_w + [(dx, dx_all)]) if dc == 0 else [],
                              inc=pe if dc == DC - 1 else None)
                        pe_mm = pe.n
                        if is_q:
                            par = evi[0] % 2
                            qb = qev_bufs[par]
                            E("a", (lambda e, bank=bank, tn=tn, qb=qb, o_g=o_g:
                                    e.activation(qb[:, 0:tn], pp[bank][:, 0:tn], AF.Identity,
                                                 bias=s_bq[:, o_g // 128: o_g // 128 + 1])),
                              waits=[(pe, pe_mm), (dqs[par], qev_free[par])], inc=act)
                            psA_free[bank] = act.n
                            DMA("s", dqs[par], q_scr[o_g:o_g + 128, t0:t0 + tn],
                                qb[:, 0:tn], waits=[(act, act.n)])
                            qev_free[par] = dqs[par].n
                        else:
                            hk = (o_g - D) // 128
                            E("a", (lambda e, bank=bank, t0=t0, tn=tn, hk=hk, o_g=o_g:
                                    e.activation(s_K[:, hk * T + t0: hk * T + t0 + tn],
                                                 pp[bank][:, 0:tn], AF.Identity,
                                                 bias=s_bq[:, o_g // 128: o_g // 128 + 1])),
                              waits=[(pe, pe_mm)], inc=act)
                            psA_free[bank] = act.n
                        evi[0] += 1
                else:
                    hv0 = o_g - 2 * D
                    for tc in range(TC):
                        bank = evi[0] % 2
                        mm_w = [(dwq[p % 2], pan_dg[p]),
                                (act, psA_free[bank])]
                        for dc in range(DC):
                            E("t", (lambda e, dc=dc, bank=bank, tc=tc, oc=oc, buf=buf:
                                    e.matmul(pp[bank][:, 0:128],
                                             s_xbf[:, dc * T + tc * 128: dc * T + tc * 128 + 128],
                                             buf[:, dc * PA + oc * 128: dc * PA + oc * 128 + 128],
                                             start=(dc == 0), stop=(dc == DC - 1))),
                              waits=(mm_w + [(dx, dx_all)]) if dc == 0 else [],
                              inc=pe if dc == DC - 1 else None)
                        pe_mm = pe.n
                        par = evi[0] % 2
                        qb = qev_bufs[par]
                        E("a", (lambda e, bank=bank, qb=qb:
                                e.activation(qb[:, 0:128], pp[bank][:, 0:128], AF.Copy)),
                          waits=[(pe, pe_mm), (dqs[par], qev_free[par])], inc=act)
                        psA_free[bank] = act.n
                        DMA("s", dqs[par], v_scr[tc * 128:(tc + 1) * 128, hv0:hv0 + 128],
                            qb[:, 0:128], waits=[(act, act.n)])
                        qev_free[par] = dqs[par].n
                        evi[0] += 1
            pan_pe_done[p] = pe.n
            if p + 2 < NPAN:
                load_panel(p + 2)

        endA = dict(pe=pe.n, act=act.n, dve=dve.n,
                    dqs0=dqs[0].n, dqs1=dqs[1].n)

        probsT_v = s_probsT[:, :].rearrange("p (c q) -> p c q", q=QBS)
        vp_v = [b[:, :].rearrange("p (c o) -> p c o", o=256) for b in vp_bufs]
        wo_v = [b[:, :].rearrange("p (c o) -> p c o", o=256) for b in wo_bufs]
        vscr_v = v_scr[:, :].rearrange("(c p) h -> p c h", p=128)
        qscr_v = q_scr[:, :].rearrange("(c p) t -> p c t", p=128)
        woutT_v = woutT[:, :].rearrange("(c p) o -> p c o", p=128)
        xT_v = xT[:, :].rearrange("(c p) t -> p c t", p=128)

        first_B = [(pe, endA["pe"]), (act, endA["act"]), (dve, endA["dve"]),
                   (dqs[0], endA["dqs0"]), (dqs[1], endA["dqs1"])]

        spill_counts = {}
        q_bufs = (s_q0, s_q1)
        pending_ln = [None]
        vp_gate = {0: endA["pe"], 1: endA["pe"]}
        wo_gate = {0: endA["pe"], 1: endA["pe"]}
        for qblk in range(NQB):
            tok0 = qblk * QBS
            dve_probs_prev = 0
            dsn_q = 0
            dve_m_prev = 0
            act_exp_prev = 0
            pe_tr_done = [0]
            pe_scores_qt = {}
            pending_tr = []

            def emit_tr(qt, dve_nrm):
                for tc in range(TC):
                    tb_ = tc % 2
                    E("t", (lambda e, tc=tc, tb_=tb_:
                            e.transpose(p_tr[tb_][:, :], s_probs[:, tc * 128:(tc + 1) * 128],
                                        s_id[:, :])),
                      waits=([(dve, dve_nrm), (act, bank_free.get(("tr", tb_), 0))]
                             if tc == 0 else [(act, bank_free.get(("tr", tb_), 0))]),
                      inc=pe)
                    E("a", (lambda e, tc=tc, qt=qt, tb_=tb_:
                            e.activation(probsT_v[:, tc, qt * 128:qt * 128 + 128],
                                         p_tr[tb_][:, :], AF.Copy)),
                      waits=[(pe, pe.n)], inc=act)
                    bank_free[("tr", tb_)] = act.n
                pe_tr_done[0] = pe.n

            NQT_B = QBS // 128
            dqr_at = {}

            def load_q(qt):
                par = qt % 2
                if qt == 0:
                    w = first_B + [(pe, bank_free.get("sq", 0))]
                elif qt == 1:
                    w = [(pe, bank_free.get("sq", 0))]
                else:
                    w = [(pe, pe_scores_qt[qt - 2])]
                DMA("s", dqr[par],
                    q_bufs[par][:, :].rearrange("p (c t) -> p c t", t=QW),
                    qscr_v[:, :, tok0 + qt * QW: tok0 + (qt + 1) * QW], waits=w)
                dqr_at[qt] = dqr[par].n

            load_q(0)
            if NQT_B > 1:
                load_q(1)
            for qt in range(NQT_B):
                qpar = qt % 2
                first_B = []
                dsn_q = dqr_at[qt]
                for ki, (k0, kn) in enumerate(KTB):
                    skb = ki % 2
                    E_w = [(act, max(bank_free.get(("scb", skb), 0),
                                     bank_free.get(("ctx", skb), 0)))]
                    for hc in range(DC):
                        E("t", (lambda e, skb=skb, hc=hc, k0=k0, kn=kn, qpar=qpar:
                                e.matmul(pp[skb][:, 0:kn],
                                         q_bufs[qpar][:, hc * QW: hc * QW + 128],
                                         s_K[:, hc * T + k0: hc * T + k0 + kn],
                                         start=(hc == 0), stop=(hc == DC - 1))),
                          waits=(([(dqr[qpar], dsn_q)] + E_w) if hc == 0 else []),
                          inc=pe if hc == DC - 1 else None)
                    E("a", (lambda e, skb=skb, k0=k0, kn=kn:
                            e.activation(s_scores[:, k0:k0 + kn], pp[skb][:, 0:kn],
                                         AF.Copy)),
                      waits=[(pe, pe.n), (dve, dve_m_prev), (act, act_exp_prev)],
                      inc=act)
                    bank_free[("scb", skb)] = act.n
                pe_scores_qt[qt] = pe.n
                if qt + 2 < NQT_B:
                    load_q(qt + 2)
                if pending_tr:
                    emit_tr(*pending_tr.pop())
                if pending_ln[0] is not None:
                    if next(pending_ln[0], "DONE") == "DONE":
                        pending_ln[0] = None
                act_sc = act.n
                E("v", (lambda e:
                        e.reduce_max(s_negm[:, 0:1], s_scores[:, :], AX.X, negate=True)),
                  waits=[(act, act_sc)], inc=dve)
                E("v", lambda e: e.tensor_scalar_mul(s_negm[:, 1:2], s_negm[:, 0:1], RSQH),
                  waits=[(dve, dve.n)], inc=dve)
                dve_m = dve.n
                dve_m_prev = dve_m
                E("a", (lambda e:
                        e.activation(s_probs[:, :], s_scores[:, :],
                                     AF.Exp, bias=s_negm[:, 1:2], scale=RSQH)),
                  waits=[(dve, dve_m), (pe, pe_tr_done[0])], inc=act)
                act_exp = act.n
                act_exp_prev = act_exp
                E("v", lambda e: e.reduce_sum(s_negm[:, 2:3], s_probs[:, :], AX.X),
                  waits=[(act, act_exp)] + ([(dqr, dqr.n)] if debug_outputs else []),
                  inc=dve)
                E("v", lambda e: e.reciprocal(s_negm[:, 3:4], s_negm[:, 2:3]),
                  waits=[(dve, dve.n)], inc=dve)
                E("v", lambda e: e.tensor_scalar_mul(s_probs[:, :], s_probs[:, :],
                                                     s_negm[:, 3:4]),
                  waits=[(dve, dve.n)], inc=dve)
                dve_nrm = dve.n
                if debug_outputs:
                    DMA("s", dqr, pb_scr[tok0 + qt * 128: tok0 + qt * 128 + 128, :],
                        s_probs[:, :], waits=[(dve, dve_nrm)])
                pending_tr.append((qt, dve_nrm))
            if pending_tr:
                emit_tr(*pending_tr.pop())
            act_probsT = act.n

            def load_vp(hcp):
                par = hcp % 2
                DMA("s", dvp[par], vp_v[par][:, :, :],
                    vscr_v[:, :, hcp * 256: hcp * 256 + 256],
                    waits=[(pe, vp_gate[par])])
                dsn_vps[hcp] = dvp[par].n

            dsn_vps = {}
            load_vp(0)
            if DC // 2 > 1:
                load_vp(1)

            if pending_ln[0] is not None:
                while next(pending_ln[0], "DONE") != "DONE":
                    pass
                pending_ln[0] = None
                bank_free["xlnfree"] = dve.n
            if qblk == NQB - 1 and NQB > 1:
                w0 = [(pe, pe.n), (dpre, pre_all),
                      (dso[0], dso[0].n), (dso[1], dso[1].n)]
                for dc in range(DC):
                    DMA("g", dacc, s_acc[:, dc * TO: dc * TO + TBS[0][1]],
                        x1_scr[dc * 128:(dc + 1) * 128, 0:TBS[0][1]], waits=w0)
                    w0 = []
                dacc_h0 = dacc.n

            for hcp in range(DC // 2):
                vpar = hcp % 2
                dsn_vp = dsn_vps[hcp]
                for hh in range(2):
                    hc = hcp * 2 + hh
                    bank = hc % 2
                    for tc in range(TC):
                        E("t", (lambda e, tc=tc, hh=hh, bank=bank, vpar=vpar:
                                e.matmul(pp[bank][:, 0:QBS],
                                         vp_v[vpar][:, tc, hh * 128: hh * 128 + 128],
                                         probsT_v[:, tc, :],
                                         start=(tc == 0), stop=(tc == TC - 1))),
                          waits=([(dvp[vpar], dsn_vp),
                                  (act, max(act_probsT, bank_free.get(("ctx", bank), 0),
                                            bank_free.get(("scb", bank), 0)))]
                                 if tc == 0 else []),
                          inc=pe if tc == TC - 1 else None)
                    E("a", (lambda e, hc=hc, bank=bank:
                            e.activation(s_ctx[:, hc * QBS:(hc + 1) * QBS],
                                         pp[bank][:, 0:QBS], AF.Identity,
                                         bias=s_bq[:, 2 * DC + hc: 2 * DC + hc + 1])),
                      waits=[(pe, pe.n)], inc=act)
                    bank_free[("ctx", bank)] = act.n
                    if debug_outputs:
                        DMA("s", dvp[vpar], ctx_scr[hc * 128:(hc + 1) * 128, tok0:tok0 + QBS],
                            s_ctx[:, hc * QBS:(hc + 1) * QBS], waits=[(act, act.n)])
                vp_gate[vpar] = pe.n
                if hcp + 2 < DC // 2:
                    load_vp(hcp + 2)
            act_ctx = act.n

            dg_wos = {}

            def load_wo(opan):
                par = opan % 2
                DMA("g", dwo[par], wo_v[par][:, :, :],
                    woutT_v[:, :, opan * 256: opan * 256 + 256],
                    waits=[(pe, wo_gate[par])])
                dg_wos[opan] = dwo[par].n

            load_wo(0)
            if DC // 2 > 1:
                load_wo(1)
            for opan in range(DC // 2):
                wpar = opan % 2
                dg_wo = dg_wos[opan]
                for oh in range(2):
                    oc = opan * 2 + oh
                    par = oc % 2
                    xow = xow_bufs[par]
                    DMA("s", dxw[par], xow[:, 0:QBS], xT_v[:, oc, tok0:tok0 + QBS],
                        waits=[(dve, bank_free.get(("xow", par), 0))])
                    dsn_x = dxw[par].n
                    bank = 2 + (oc % 2)
                    for hc in range(DC):
                        E("t", (lambda e, hc=hc, oh=oh, bank=bank, wpar=wpar:
                                e.matmul(pp[bank][:, 0:QBS],
                                         wo_v[wpar][:, hc, oh * 128: oh * 128 + 128],
                                         s_ctx[:, hc * QBS:(hc + 1) * QBS],
                                         start=(hc == 0), stop=(hc == DC - 1))),
                          waits=([(dwo[wpar], dg_wo),
                                  (act, max(act_ctx, bank_free.get(("wo", bank), 0))),
                                  (dve, bank_free.get(("lnbc", 0), 0))]
                                 if hc == 0 else []),
                          inc=pe if hc == DC - 1 else None)
                    E("a", (lambda e, oc=oc, bank=bank:
                            e.activation(s_xln[:, oc * QBS:(oc + 1) * QBS],
                                         pp[bank][:, 0:QBS], AF.Identity,
                                         bias=s_bo[:, oc:oc + 1])),
                      waits=[(pe, pe.n), (dve, bank_free.get("xlnfree", 0)),
                             (po, bank_free.get("xlnfree_po", 0))], inc=act)
                    bank_free[("wo", bank)] = act.n
                    E("v", (lambda e, oc=oc, xow=xow:
                            e.tensor_add(s_xln[:, oc * QBS:(oc + 1) * QBS],
                                         s_xln[:, oc * QBS:(oc + 1) * QBS],
                                         xow[:, 0:QBS])),
                      waits=[(act, act.n), (dxw[par], dsn_x)], inc=dve)
                    bank_free[("xow", par)] = dve.n
                    if debug_outputs:
                        DMA("s", dxw[par], h1_scr[oc * 128:(oc + 1) * 128, tok0:tok0 + QBS],
                            s_xln[:, oc * QBS:(oc + 1) * QBS], waits=[(dve, dve.n)])
                wo_gate[wpar] = pe.n
                if opan + 2 < DC // 2:
                    load_wo(opan + 2)
            dve_h1 = dve.n
            if qblk == NQB - 1:
                pe_wo_end = pe.n
                dve_h1_end = dve_h1

            if qblk == NQB - 1:
                assert pending_ln[0] is None
                ln1_apply_dve = run_ln(s_xln, 0, QBS, dve_h1, None, tok0, QBS,
                                       acc_dst=s_acc, acc_stride=TO)
            else:
                pending_ln[0] = layernorm_gen(s_xln, 0, QBS, dve_h1, x1_scr,
                                              tok0, QBS)
                next(pending_ln[0], None)
            bank_free["xlnfree"] = dve.n
            bank_free["xlnfree_po"] = po.n
            bank_free["sq"] = pe.n

        endB = dict(pe=pe.n, act=act.n, dve=dve.n, po=po.n,
                    dso0=dso[0].n, dso1=dso[1].n)

        dsn_acc = dacc.n
        first_x = [(pe, pe_wo_end), (act, endB["act"])]
        act_xlnb_dc = {}
        last_t0 = TBS[-1][0] if NQB > 1 else None
        for (t0r, tnr) in TBS:
            for dc in range(DC):
                w = list(first_x)
                first_x = []
                if t0r == last_t0 or NQB == 1:
                    w.append((dve, ln1_apply_dve[dc]))
                else:
                    w.append((dacc, dsn_acc))
                E("a", (lambda e, dc=dc, t0r=t0r, tnr=tnr:
                        e.activation(s_xlnb[:, dc * TO + t0r: dc * TO + t0r + tnr],
                                     s_acc[:, dc * TO + t0r: dc * TO + t0r + tnr], AF.Copy)),
                  waits=w, inc=act)
                act_xlnb_dc[(t0r, dc)] = act.n
        act_xlnb = act.n

        aup_v = s_aup[:, :].rearrange("p (c r) -> p c r", r=R)
        adn_v = s_adn[:, :].rearrange("p (c r) -> p c r", r=R)

        t1_emitted = [False]
        dve_t1_cell = [0]

        def emit_t1():
            for bi, (t0, tn) in enumerate(TBS):
                for dc in range(DC):
                    E("t", (lambda e, dc=dc, bi=bi, t0=t0, tn=tn:
                            e.matmul(pst[bi][0:R, 0:tn], aup_v[:, dc, :],
                                     s_xlnb[:, dc * TO + t0: dc * TO + t0 + tn],
                                     start=(dc == 0), stop=(dc == DC - 1),
                                     skip_group_check=True)),
                      waits=([(act, act_xlnb_dc[(t0, dc)])] +
                             ([(dve, bank_free.get("lnrows", 0))] if dc == 0 else [])),
                      inc=pe if dc == DC - 1 else None)
                E("v", (lambda e, bi=bi, t0=t0, tn=tn:
                        e.tensor_copy(s_t1[0:R, t0:t0 + tn], pst[bi][0:R, 0:tn])),
                  waits=[(pe, pe.n)], inc=dve)
            dve_t1_cell[0] = dve.n
            t1_emitted[0] = True

        uq_v = [b[:, :].rearrange("p (c o) -> p c o", o=128) for b in uq_bufs]
        dq_v = [b[:, :].rearrange("p (c o) -> p c o", o=128) for b in dq_bufs]
        wupq_v = wupqT[:, :].rearrange("(c p) h -> p c h", p=128)
        sexpu_v = sexpu[:, :].rearrange("(c p) h -> p c h", p=128)
        dnq_v = wdnqT[:, :].rearrange("(c p) d -> p c d", p=128)
        sexpd_v = sexpd[:, :].rearrange("(c p) d -> p c d", p=128)

        up_dg, up_dve, up_pe_done = {}, {}, {}
        dn_dg, dn_dve, dn_pe_done = {}, {}, {}

        NBUF = 3

        def load_up(u):
            w = ([(pe, up_pe_done[u - NBUF])] if u - NBUF in up_pe_done else
                 [(pe, pe_wo_end), (dve, dve_h1_end)])
            DMA("g", duq[u % NBUF], uq_v[u % NBUF][:, :, :],
                wupq_v[:, :, u * 128:(u + 1) * 128], waits=w)
            DMA("g", dus[u % NBUF],
                us_bufs[u % NBUF][:, :].rearrange("p (c o) -> p c o", o=128),
                sexpu_v[:, :, u * 128:(u + 1) * 128])
            up_dg[u] = (duq[u % NBUF].n, dus[u % NBUF].n)

        def deq_up(u, eng="v"):
            esem = dve if eng == "v" else po
            if eng == "v":
                E(eng, (lambda e, u=u:
                        e.scalar_tensor_tensor(uq_bufs[u % NBUF][:, :], uq_bufs[u % NBUF][:, :],
                                               7.5, us_bufs[u % NBUF][:, :],
                                               ALU.subtract, ALU.mult)),
                  waits=[(duq[u % NBUF], up_dg[u][0]), (dus[u % NBUF], up_dg[u][1])],
                  inc=esem)
            else:
                E(eng, (lambda e, u=u:
                        e.tensor_scalar_add(uq_bufs[u % NBUF][:, :],
                                            uq_bufs[u % NBUF][:, :], -7.5)),
                  waits=[(duq[u % NBUF], up_dg[u][0])], inc=esem)
                E(eng, (lambda e, u=u:
                        e.tensor_mul(uq_bufs[u % NBUF][:, :], uq_bufs[u % NBUF][:, :],
                                     us_bufs[u % NBUF][:, :])),
                  waits=[(dus[u % NBUF], up_dg[u][1]), (esem, esem.n)], inc=esem)
            up_dve[u] = (esem, esem.n)

        def load_dn(gi, dc):
            key = gi * DC + dc
            w = ([(pe, dn_pe_done[key - NBUF])] if key - NBUF in dn_pe_done else
                 [(pe, endB["pe"]), (act, endB["act"]), (dve, endB["dve"]),
                  (po, endB["po"])])
            DMA("g", ddn[key % NBUF], dq_v[key % NBUF][:, :, :],
                dnq_v[:, gi * GS:(gi + 1) * GS, dc * 128:(dc + 1) * 128], waits=w)
            DMA("g", dds[key % NBUF],
                ds_bufs[key % NBUF][:, :].rearrange("p (c o) -> p c o", o=128),
                sexpd_v[:, gi * GS:(gi + 1) * GS, dc * 128:(dc + 1) * 128])
            dn_dg[key] = (ddn[key % NBUF].n, dds[key % NBUF].n)

        def deq_dn(key):
            E("v", (lambda e, key=key:
                    e.scalar_tensor_tensor(dq_bufs[key % NBUF][:, :], dq_bufs[key % NBUF][:, :],
                                           7.5, ds_bufs[key % NBUF][:, :],
                                           ALU.subtract, ALU.mult)),
              waits=[(ddn[key % NBUF], dn_dg[key][0]), (dds[key % NBUF], dn_dg[key][1])],
              inc=dve)
            dn_dve[key] = dve.n

        for u0 in range(min(NBUF, GS * NG)):
            load_up(u0)

        E("v", lambda e: e.memset(s_bupl[:, :], 0.0),
          waits=[(pe, endB["pe"]), (act, endB["act"]), (dve, endB["dve"]),
                 (po, endB["po"])], inc=dve)
        E("v", lambda e: e.memset(s_bdnl[:, :], 0.0), waits=[(dve, dve.n)], inc=dve)
        dve_lbz = dve.n
        DMA("g", dlb, s_bupl[0:R, :], bupl[:, :],
            waits=[(dve, dve_lbz), (dpre, pre_all)])
        DMA("g", dlb, s_bdnl[0:R, :], bdnl[:, :])
        dlb_all = dlb.n


        for gi in range(NG):
            for i in range(GS):
                u = gi * GS + i
                if u not in up_dve:
                    deq_up(u)
                if u + 1 < GS * NG and (u + 1) not in up_dve:
                    deq_up(u + 1)
                for bi, (t0, tn) in enumerate(TBS):
                    bank = (len(TBS) * u + bi) % 4
                    for dc in range(DC):
                        w_mm = []
                        if dc == 0:
                            w_mm = [up_dve[u],
                                    (dve, bank_free.get(("ppV", bank), 0)),
                                    (act, max(bank_free.get(("ppA", bank), 0),
                                              endB["act"] if u < 2 else 0))]
                        if not t1_emitted[0]:
                            w_mm = w_mm + [(act, act_xlnb_dc[(t0, dc)])]
                        E("t", (lambda e, dc=dc, u=u, bank=bank, t0=t0, tn=tn:
                                e.matmul(pp[bank][:, 0:tn], uq_v[u % NBUF][:, dc, :],
                                         s_xlnb[:, dc * TO + t0: dc * TO + t0 + tn],
                                         start=(dc == 0), stop=False)),
                          waits=w_mm)
                    if not t1_emitted[0]:
                        emit_t1()
                    E("t", (lambda e, u=u, bank=bank, t0=t0, tn=tn:
                            e.matmul(pp[bank][:, 0:tn],
                                     s_bupl[:, u * 128:(u + 1) * 128],
                                     s_t1[:, t0:t0 + tn], start=False, stop=True)),
                      waits=([(dlb, dlb_all), (dve, dve_t1_cell[0])]
                             if u == 0 else []), inc=pe)
                    E("a", (lambda e, i=i, u=u, bank=bank, t0=t0, tn=tn:
                            e.activation(s_x2[:, i * TO + t0: i * TO + t0 + tn],
                                         pp[bank][:, 0:tn], AF.Relu,
                                         bias=s_bu[:, u:u + 1])),
                      waits=[(pe, pe.n)], inc=act)
                    bank_free[("ppA", bank)] = act.n
                up_pe_done[u] = pe.n
                if u + NBUF < GS * NG:
                    load_up(u + NBUF)
            act_x2 = act.n

            for bi, (t0, tn) in enumerate(TBS):
                for i in range(GS):
                    hcg = gi * GS + i
                    E("t", (lambda e, i=i, hcg=hcg, bi=bi, t0=t0, tn=tn, gi=gi:
                            e.matmul(pst[bi][0:R, 0:tn], adn_v[:, hcg, :],
                                     s_x2[:, i * TO + t0: i * TO + t0 + tn],
                                     start=(gi == 0 and i == 0),
                                     stop=(gi == NG - 1 and i == GS - 1),
                                     skip_group_check=True)),
                      waits=([(act, act_x2), (dve, dve_t1_cell[0])] if i == 0 else []),
                      inc=pe if (gi == NG - 1 and i == GS - 1 and bi == len(TBS) - 1) else None)

            if gi == 0:
                for d0 in range(min(NBUF, DC)):
                    load_dn(0, d0)
            for dc in range(DC):
                key = gi * DC + dc
                if dc == 0 and gi == 0:
                    deq_dn(key)
                for k2 in (key + 1, key + 2):
                    if k2 < NG * DC and k2 <= key + NBUF - 1 and                        k2 > max(dn_dve, default=-1):
                        deq_dn(k2)
                for bi, (t0, tn) in enumerate(TBS):
                    bank = (len(TBS) * dc + bi) % 4
                    for i in range(GS):
                        E("t", (lambda e, i=i, key=key, bank=bank, t0=t0, tn=tn:
                                e.matmul(pp[bank][:, 0:tn], dq_v[key % NBUF][:, i, :],
                                         s_x2[:, i * TO + t0: i * TO + t0 + tn],
                                         start=(i == 0), stop=(i == GS - 1))),
                          waits=([(act, max(act_x2, bank_free.get(("ppA", bank), 0))),
                                  (dve, max(dn_dve[key],
                                            bank_free.get(("ppV", bank), 0)))]
                                 if i == 0 else []),
                          inc=pe if i == GS - 1 else None)
                    E("v", (lambda e, dc=dc, bank=bank, t0=t0, tn=tn:
                            e.tensor_add(s_acc[:, dc * TO + t0: dc * TO + t0 + tn],
                                         pp[bank][:, 0:tn],
                                         s_acc[:, dc * TO + t0: dc * TO + t0 + tn])),
                      waits=[(pe, pe.n), (dacc, dsn_acc)], inc=dve)
                    bank_free[("ppV", bank)] = dve.n
                dn_pe_done[key] = pe.n
                if dc + NBUF < DC:
                    load_dn(gi, dc + NBUF)
                elif gi + 1 < NG and dc + NBUF - DC < min(NBUF, DC):
                    load_dn(gi + 1, dc + NBUF - DC)

        acc_done = {}
        for bi, (t0, tn) in enumerate(TBS):
            E("v", (lambda e, bi=bi, t0=t0, tn=tn:
                    e.tensor_copy(s_t2[0:R, t0:t0 + tn], pst[bi][0:R, 0:tn])),
              waits=[(pe, pe.n)], inc=dve)
        dve_t2 = dve.n
        for dc in range(DC):
            for bi, (t0, tn) in enumerate(TBS):
                bank = (len(TBS) * dc + bi) % 4
                E("t", (lambda e, dc=dc, bank=bank, t0=t0, tn=tn:
                        e.matmul(pp[bank][:, 0:tn], s_bdnl[:, dc * 128:(dc + 1) * 128],
                                 s_t2[:, t0:t0 + tn], start=True, stop=True)),
                  waits=[(dve, max(dve_t2, bank_free.get(("ppV", bank), 0),
                                   bank_free.get(("lb", bank), 0))),
                         (act, bank_free.get(("ppA", bank), 0)),
                         (dlb, dlb_all)], inc=pe)
                E("v", (lambda e, dc=dc, bank=bank, t0=t0, tn=tn:
                        e.scalar_tensor_tensor(
                            s_acc[:, dc * TO + t0: dc * TO + t0 + tn],
                            pp[bank][:, 0:tn], s_bd[:, dc:dc + 1],
                            s_acc[:, dc * TO + t0: dc * TO + t0 + tn],
                            ALU.add, ALU.add)),
                  waits=[(pe, pe.n)], inc=dve)
                bank_free[("lb", bank)] = dve.n
                bank_free[("ppV", bank)] = dve.n
            acc_done[dc] = (dve, dve.n)
        dve_acc = dve.n

        for (t0, tn) in TBS:
            run_ln(s_acc, t0, tn, dve_acc, y, t0, TO, ready_map=acc_done)

        E("s", None, waits=[(dso[0], dso[0].n), (dso[1], dso[1].n)])

        def replay(key):
            def f(eng):
                for fn, waits, incs in ops[key]:
                    for h, v in waits:
                        eng.wait_ge(h, v)
                    if fn is not None:
                        ins = fn(eng)
                        for h, d in incs:
                            ins.then_inc(h, d)
            return f

        with nc.Block() as blk:
            blk.gpsimd(replay("g"))
            blk.sync(replay("s"))
            blk.tensor(replay("t"))
            blk.vector(replay("v"))
            blk.scalar(replay("a"))

    return nc


def make_core_inputs(cfg, x_b, w_qkv, b_qkv, w_out, b_out, w_up_q, w_up_scale,
                     b_up, a_up, b_upl, w_dn_q, w_dn_scale, b_dn, a_dn, b_dnl,
                     half, shared):
    D, T, TO, FF, R = cfg["D"], cfg["T"], cfg["TO"], cfg["FF"], cfg["R"]
    f32 = np.float32
    xTb = np.ascontiguousarray(x_b.T.astype(f32, copy=False))
    if half == 1:
        xTb = np.ascontiguousarray(
            np.concatenate([xTb[:, TO:], xTb[:, :TO]], axis=1))
    ins = {"xT": xTb}
    ins.update(shared)
    return ins


def make_shared_inputs(cfg, w_qkv, b_qkv, w_out, b_out, w_up_q, w_up_scale,
                       b_up, a_up, b_upl, w_dn_q, w_dn_scale, b_dn, a_dn,
                       b_dnl):
    D, FF = cfg["D"], cfg["FF"]
    BLK = 64
    f32 = np.float32

    def pp_(v, n):
        return np.ascontiguousarray(np.asarray(v, dtype=f32).reshape(n, 128).T)

    return {
        "wqkvT": np.ascontiguousarray(np.asarray(w_qkv, dtype=f32).T),
        "woutT": np.ascontiguousarray(np.asarray(w_out, dtype=f32).T),
        "bqkv_pp": pp_(b_qkv, 3 * D // 128),
        "bout_pp": pp_(b_out, D // 128),
        "wupqT": np.ascontiguousarray(np.asarray(w_up_q, dtype=np.int32).T).astype(np.int8),
        "sexpu": np.ascontiguousarray(
            np.repeat(np.asarray(w_up_scale, dtype=f32).T, BLK, axis=0)),
        "bup_pp": pp_(b_up, FF // 128),
        "aup": np.ascontiguousarray(np.asarray(a_up, dtype=f32)),
        "bupl": np.ascontiguousarray(np.asarray(b_upl, dtype=f32)),
        "wdnqT": np.ascontiguousarray(np.asarray(w_dn_q, dtype=np.int32).T).astype(np.int8),
        "sexpd": np.ascontiguousarray(
            np.repeat(np.asarray(w_dn_scale, dtype=f32).T, BLK, axis=0)),
        "bdn_pp": pp_(b_dn, D // 128),
        "adn": np.ascontiguousarray(np.asarray(a_dn, dtype=f32)),
        "bdnl": np.ascontiguousarray(np.asarray(b_dnl, dtype=f32)),
        "ident": np.eye(128, dtype=f32),
    }


def kernel(L0_attn_hidden_states, L0_attn_w_qkv, L0_attn_b_qkv, L0_attn_w_out,
           L0_attn_b_out, L0_ffn_w_up_q, L0_ffn_w_up_scale, L0_ffn_b_up,
           L0_ffn_w_up_lora_a, L0_ffn_w_up_lora_b, L0_ffn_w_down_q,
           L0_ffn_w_down_scale, L0_ffn_b_down, L0_ffn_w_down_lora_a,
           L0_ffn_w_down_lora_b):
    cfg = CFG_FULL
    D, T, TO = cfg["D"], cfg["T"], cfg["TO"]
    x = np.asarray(L0_attn_hidden_states, dtype=np.float32)
    B = x.shape[0]

    nc = build_program(cfg)
    shared = make_shared_inputs(
        cfg, L0_attn_w_qkv, L0_attn_b_qkv, L0_attn_w_out, L0_attn_b_out,
        L0_ffn_w_up_q, L0_ffn_w_up_scale, L0_ffn_b_up, L0_ffn_w_up_lora_a,
        L0_ffn_w_up_lora_b, L0_ffn_w_down_q, L0_ffn_w_down_scale,
        L0_ffn_b_down, L0_ffn_w_down_lora_a, L0_ffn_w_down_lora_b)

    in_maps = []
    for core in range(8):
        b, half = core // 2, core % 2
        in_maps.append(make_core_inputs(
            cfg, x[b], None, None, None, None, None, None, None, None, None,
            None, None, None, None, None, half, shared))

    res = run_bass_kernel_spmd(nc, in_maps, core_ids=list(range(8)))

    out = np.empty((B, T, D), dtype=np.float32)
    for core in range(8):
        b, half = core // 2, core % 2
        out[b, half * TO:(half + 1) * TO, :] = res.results[core]["y"].T
    return out
